# version 02
# baseline (speedup 1.0000x reference)
import numpy as np
import ml_dtypes
import concourse.bass as bass
import concourse.mybir as mybir
from concourse.alu_op_type import AluOpType as OP
from concourse.bass_utils import run_bass_kernel_spmd
from contextlib import ExitStack

F32 = mybir.dt.float32
BF16 = mybir.dt.bfloat16
AF = mybir.ActivationFunctionType
AX = mybir.AxisListType

ENGS = ("sync", "gpsimd", "vector", "scalar", "tensor")

B, T, D = 4, 8192, 2048
HLOC = 8
NKC = 16
NKC_C = 8
TTILE = 512
NTA = T // TTILE
NSTA = 4
NSTEP = HLOC * 4
THALF = T // 2
TS = 2048
NST_D = THALF // TS
NFC = 11
NE = 18
EPS_LN = 1e-5
EPS_DEN = 1e-6
INV_D = 1.0 / D
BIG = 1.0e30


class Sem:
    def __init__(self, h):
        self.h = h
        self.n = 0


class Emitter:
    def __init__(self):
        self.q = {k: [] for k in ENGS}

    def op(self, eng, f, inc=None, amt=1):
        if inc is not None:
            inc.n += amt
            self.q[eng].append(lambda E, f=f, s=inc.h, a=amt: f(E).then_inc(s, a))
            return inc.n
        self.q[eng].append(f)
        return None

    def wait(self, eng, sem, val):
        if val is None or val <= 0:
            return
        self.q[eng].append(lambda E, s=sem.h, v=val: E.wait_ge(s, v))

    def register(self, block):
        for eng in ENGS:
            ops = self.q[eng]

            def run(E, ops=ops):
                for f in ops:
                    f(E)
            getattr(block, eng)(run)


def build_nc(debug=False, phases='ACD'):
    nc = bass.Bass("TRN2", target_bir_lowering=False)

    xa_h = nc.dram_tensor("xa", [T, D], F32, kind="ExternalInput")
    xc_h = nc.dram_tensor("xc", [THALF, D], F32, kind="ExternalInput")
    wqkv_h = nc.dram_tensor("wqkv", [HLOC, 128, NKC * 384], F32,
                            kind="ExternalInput")
    wout_h = nc.dram_tensor("wout", [128, NKC_C * D], F32, kind="ExternalInput")
    gatew_h = nc.dram_tensor("gatew", [128, NKC * 16], F32, kind="ExternalInput")
    wff_hs = [nc.dram_tensor(f"wff{e}", [NFC, 128, NKC * 2 * 128], F32,
                             kind="ExternalInput") for e in range(NE)]
    w2t_hs = [nc.dram_tensor(f"w2t{e}", [128, 4 * NFC * 512], F32,
                             kind="ExternalInput") for e in range(NE)]
    dp_h = nc.dram_tensor("dp", [128, HLOC], F32, kind="ExternalInput")
    ident_h = nc.dram_tensor("identc", [128, 128], BF16, kind="ExternalInput")
    onec_h = nc.dram_tensor("onec", [128, 1], BF16, kind="ExternalInput")
    oner_h = nc.dram_tensor("oner", [1, 128], BF16, kind="ExternalInput")

    out_h = nc.dram_tensor("out", [THALF, D], F32, kind="ExternalOutput")

    o_mine = nc.dram_tensor("o_mine", [HLOC * 128, T], BF16)
    partial_h = nc.dram_tensor("partial", [T, D], BF16)
    attnf_h = nc.dram_tensor("attnf", [THALF, D], BF16)
    xmid_h = nc.dram_tensor("xmid", [THALF, D], F32)
    y_h = nc.dram_tensor("ybuf", [NST_D, TS, D], F32)

    dbg = {}
    if debug:
        dbg["omine"] = nc.dram_tensor("dbg_omine", [HLOC * 128, T], BF16,
                                      kind="ExternalOutput")
        dbg["attnf"] = nc.dram_tensor("dbg_attnf", [THALF, D], BF16,
                                      kind="ExternalOutput")
        dbg["xmid"] = nc.dram_tensor("dbg_xmid", [THALF, D], F32,
                                     kind="ExternalOutput")
        dbg["y"] = nc.dram_tensor("dbg_y", [NST_D, TS, D], F32,
                                  kind="ExternalOutput")

    with ExitStack() as G:
        eg = G.enter_context
        sems = {k: Sem(eg(nc.semaphore(f"s_{k}")))
                for k in ("xl", "wd", "st", "pe", "ve", "se", "cc")}
        xl, wd, st = sems["xl"], sems["wd"], sems["st"]
        pe, ve, se, cc = sems["pe"], sems["ve"], sems["se"], sems["cc"]

        ident = eg(nc.sbuf_tensor("identb", [128, 128], BF16))
        onec = eg(nc.sbuf_tensor("onecb", [128, 1], BF16))
        oner = eg(nc.sbuf_tensor("onerb", [1, 128], BF16))

        def ln_plan(em, X, OUTp, J, S, so, pre_waits=()):
            for eng, sm, val in pre_waits:
                em.wait(eng, sm, val)
            c = em.op("vector", lambda E: E.tensor_reduce(
                S[:, so:so + 1], X, AX.X, OP.add), inc=ve)
            s1 = em.op("scalar", lambda E: E.activation(
                out=J, in_=X, func=AF.Square, accum_out=S[:, so + 1:so + 2]),
                inc=se)
            em.wait("vector", ve, c)
            c = em.op("vector", lambda E: E.tensor_scalar(
                S[:, so + 2:so + 3], S[:, so:so + 1], INV_D, None, OP.mult),
                inc=ve)
            em.wait("vector", ve, c)
            c = em.op("vector", lambda E: E.tensor_tensor(
                S[:, so + 3:so + 4], S[:, so + 2:so + 3], S[:, so + 2:so + 3],
                OP.mult), inc=ve)
            em.wait("vector", ve, c)
            c = em.op("vector", lambda E: E.tensor_scalar(
                S[:, so + 3:so + 4], S[:, so + 3:so + 4], EPS_LN, None,
                OP.subtract), inc=ve)
            em.wait("vector", se, s1)
            em.wait("vector", ve, c)
            v_var = em.op("vector", lambda E: E.scalar_tensor_tensor(
                S[:, so + 3:so + 4], S[:, so + 1:so + 2], INV_D,
                S[:, so + 3:so + 4], OP.mult, OP.subtract), inc=ve)
            em.wait("scalar", ve, v_var)
            c = em.op("scalar", lambda E: E.activation(
                out=S[:, so + 4:so + 5], in_=S[:, so + 3:so + 4], func=AF.Ln),
                inc=se)
            em.wait("scalar", se, c)
            s2 = em.op("scalar", lambda E: E.activation(
                out=S[:, so + 4:so + 5], in_=S[:, so + 4:so + 5], func=AF.Exp,
                scale=-0.5), inc=se)
            em.wait("vector", se, s2)
            v_norm = em.op("vector", lambda E: E.tensor_scalar(
                OUTp, X, S[:, so + 2:so + 3], S[:, so + 4:so + 5],
                OP.subtract, OP.mult), inc=ve)
            return v_norm, s1

        with ExitStack() as SA:
            ea = SA.enter_context
            dpt = ea(nc.sbuf_tensor("A_dp", [128, HLOC], F32))
            sdec = ea(nc.sbuf_tensor("A_sdec", [128, HLOC], F32))
            xbuf = ea(nc.sbuf_tensor("A_x", [128, 2 * D], F32))
            lntm = ea(nc.sbuf_tensor("A_lntm", [128, 2 * D], BF16))
            lnfm = ea(nc.sbuf_tensor("A_lnfm", [128, NKC * TS], BF16))
            whead = ea(nc.sbuf_tensor("A_wh", [128, 2 * NKC * 384], BF16))
            dect = ea(nc.sbuf_tensor("A_dect", [128, HLOC * TTILE], F32))
            kvc = ea(nc.sbuf_tensor("A_kvc", [128, HLOC], F32))
            kcc = ea(nc.sbuf_tensor("A_kcc", [128, HLOC], F32))
            stat = ea(nc.sbuf_tensor("A_stat", [128, 12], F32))
            phiq = ea(nc.sbuf_tensor("A_phiq", [128, 4 * TTILE], BF16))
            phik = ea(nc.sbuf_tensor("A_phik", [128, 2 * TTILE], BF16))
            kvb = ea(nc.sbuf_tensor("A_kv", [128, 2 * TTILE], BF16))
            kvs = ea(nc.sbuf_tensor("A_kvs", [128, 4 * TTILE], F32))
            kss = ea(nc.sbuf_tensor("A_kss", [128, 2 * TTILE], F32))
            prodb = ea(nc.sbuf_tensor("A_prod", [128, 2 * TTILE], BF16))
            eqb = ea(nc.sbuf_tensor("A_eq", [128, 2 * TTILE], F32))
            ekb = ea(nc.sbuf_tensor("A_ek", [128, 2 * TTILE], F32))
            t0b = ea(nc.sbuf_tensor("A_t0", [128, TTILE], BF16))
            dencb = ea(nc.sbuf_tensor("A_denc", [1, 2 * TTILE], F32))
            rb16 = ea(nc.sbuf_tensor("A_rb16", [1, 2 * TTILE], BF16))
            obuf = ea(nc.sbuf_tensor("A_obuf", [128, 4 * TTILE], BF16))
            padt = ea(nc.sbuf_tensor("A_pad", [128, 8], F32))
            pbig0 = ea(nc.psum_tensor("A_p0", [128, 2048], F32))
            pbig1 = ea(nc.psum_tensor("A_p1", [128, 2048], F32))
            tpV = [pbig1[:, k * 512:(k + 1) * 512].bitcast(BF16)
                   for k in range(4)]

            em = Emitter()

            em.op("sync", lambda E: E.dma_start(out=ident[:, :],
                                                in_=ident_h[:, :]),
                  inc=xl, amt=16)
            em.op("sync", lambda E: E.dma_start(out=onec[:, :],
                                                in_=onec_h[:, :]),
                  inc=xl, amt=16)
            em.op("sync", lambda E: E.dma_start(out=oner[:, :],
                                                in_=oner_h[:, :]),
                  inc=xl, amt=16)
            c_dp = em.op("sync", lambda E: E.dma_start(out=dpt[:, :],
                                                       in_=dp_h[:, :]),
                         inc=xl, amt=16)
            em.wait("scalar", xl, c_dp)
            s_e = em.op("scalar", lambda E: E.activation(
                out=sdec[:, :], in_=dpt[:, :], func=AF.Exp, scale=-1.0), inc=se)
            em.wait("vector", se, s_e)
            c = em.op("vector", lambda E: E.tensor_scalar(
                sdec[:, :], sdec[:, :], 1.0, None, OP.add), inc=ve)
            em.wait("vector", ve, c)
            c = em.op("vector", lambda E: E.reciprocal(sdec[:, :], sdec[:, :]),
                      inc=ve)
            em.wait("vector", ve, c)
            em.op("vector", lambda E: E.memset(dect[:, :], 0.0))
            em.op("vector", lambda E: E.memset(kvc[:, :], 0.0))
            em.op("vector", lambda E: E.memset(kcc[:, :], 0.0))
            for h in range(HLOC):
                em.op("vector", lambda E, h=h: E.tensor_scalar(
                    dect[:, h * TTILE:(h + 1) * TTILE],
                    dect[:, h * TTILE:(h + 1) * TTILE],
                    sdec[:, h:h + 1], None, OP.add))

            def wh_dma(E, ghi):
                h = ghi % HLOC
                slot = ghi % 2
                return E.dma_start(
                    out=whead[:, slot * NKC * 384:(slot + 1) * NKC * 384],
                    in_=wqkv_h[h, :, :])
            c_wh = {0: em.op("gpsimd", lambda E: wh_dma(E, 0), inc=wd, amt=16),
                    1: em.op("gpsimd", lambda E: wh_dma(E, 1), inc=wd, amt=16)}

            def x_dma(E, gc):
                slot = gc % 2
                r = gc * 128
                return E.dma_start(out=xbuf[:, slot * D:(slot + 1) * D],
                                   in_=xa_h[r:r + 128, :])
            c_xl = {0: em.op("sync", lambda E: x_dma(E, 0), inc=xl, amt=16),
                    1: em.op("sync", lambda E: x_dma(E, 1), inc=xl, amt=16)}

            cp_done_g = {}
            cp_done_tile = {}
            tp_pe = {}
            qcnt = {}
            dencnt = {}
            rbcnt = {}
            kv_abs = {}
            prod_abs = {}
            denc_abs = {}
            rb16_abs = {}
            omul_abs = {}
            odma_abs = {}
            eq_abs = {}

            def plan_prep_tile(st_a, tl):
                gt = st_a * 4 + tl
                for c in range(4):
                    gc = gt * 4 + c
                    slot = gc % 2
                    X = xbuf[:, slot * D:(slot + 1) * D]
                    OUTt = lntm[:, slot * D:(slot + 1) * D]
                    so = (gc % 2) * 6
                    pre = [("vector", xl, c_xl[gc]), ("scalar", xl, c_xl[gc])]
                    if (gc - 2, NKC - 1) in tp_pe:
                        pre.append(("scalar", pe, tp_pe[(gc - 2, NKC - 1)]))
                    v2, s1 = ln_plan(em, X, OUTt, OUTt, stat, so, pre)
                    if gc + 2 < NTA * 4:
                        em.wait("sync", ve, v2)
                        em.wait("sync", se, s1)
                        c_xl[gc + 2] = em.op(
                            "sync", lambda E, g=gc + 2: x_dma(E, g),
                            inc=xl, amt=16)
                    em.wait("tensor", ve, v2)
                    if gc == st_a * 16 and st_a > 0:
                        em.wait("tensor", ve, kv_abs[(st_a - 1, NSTEP - 1)])
                        em.wait("tensor", ve, omul_abs[(st_a - 1, NSTEP - 1)])
                    toff = tl * 512 + c * 128
                    for kc in range(NKC):
                        prevk = (gc, kc - 4) if kc >= 4 else (gc - 1, NKC - 4 + kc)
                        if prevk in cp_done_g:
                            em.wait("tensor", ve, cp_done_g[prevk])
                        tpv = tpV[kc % 4]
                        f = (lambda E, slot=slot, kc=kc, tpv=tpv: E.transpose(
                            tpv[:, 0:128],
                            lntm[:, slot * D + kc * 128:
                                 slot * D + (kc + 1) * 128],
                            ident[:, :]))
                        tp_pe[(gc, kc)] = em.op("tensor", f, inc=pe)
                        em.wait("vector", pe, tp_pe[(gc, kc)])
                        dst = lnfm[:, kc * TS + toff:kc * TS + toff + 128]
                        cp_done_g[(gc, kc)] = em.op(
                            "vector", lambda E, tpv=tpv, dst=dst:
                            E.tensor_copy(dst, tpv[:, 0:128]), inc=ve)
                    cp_done_tile[(st_a, tl)] = cp_done_g[(gc, NKC - 1)]

            def plan_heads(st_a):
                for it in range(NSTEP + 3):
                    s, sm1, sm2, sm3 = it, it - 1, it - 2, it - 3

                    if s < NSTEP:
                        h, tl = divmod(s, 4)
                        ghi = st_a * HLOC + h
                        pq = pbig0 if s % 2 == 0 else pbig1
                        em.wait("tensor", ve, cp_done_tile[(st_a, tl)])
                        if s == 1:
                            em.wait("tensor", ve, cp_done_tile[(st_a, 3)])
                        em.wait("tensor", wd, c_wh[ghi])
                        prev_user = (st_a, s - 2) if s - 2 >= 0 else \
                            ((st_a - 1, NSTEP - 2 + s) if st_a > 0 else None)
                        if prev_user in kv_abs:
                            em.wait("tensor", ve, kv_abs[prev_user])
                        wslot = ghi % 2
                        cnts = []
                        for fc3 in range(3):
                            for kc in range(NKC):
                                f = (lambda E, pq=pq, fc3=fc3, kc=kc,
                                     wslot=wslot, tl=tl: E.matmul(
                                         pq[:, fc3 * 512:(fc3 + 1) * 512],
                                         whead[:, wslot * NKC * 384 +
                                               (kc * 3 + fc3) * 128:
                                               wslot * NKC * 384 +
                                               (kc * 3 + fc3) * 128 + 128],
                                         lnfm[:, kc * TS + tl * 512:
                                              kc * TS + (tl + 1) * 512],
                                         start=(kc == 0), stop=(kc == NKC - 1)))
                                if kc == NKC - 1:
                                    cnts.append(em.op("tensor", f, inc=pe))
                                else:
                                    em.op("tensor", f)
                        qcnt[(st_a, s)] = cnts
                    if 0 <= sm1 < NSTEP:
                        pq = pbig0 if sm1 % 2 == 0 else pbig1
                        esl = sm1 % 2
                        prev_phi = (st_a, sm1 - 2) if sm1 - 2 >= 0 else \
                            ((st_a - 1, NSTEP - 2 + sm1) if st_a > 0 else None)
                        if prev_phi in kv_abs:
                            em.wait("scalar", ve, kv_abs[prev_phi])
                        em.wait("scalar", pe, qcnt[(st_a, sm1)][1])
                        em.op("scalar", lambda E, pq=pq, esl=esl: E.activation(
                            out=eqb[:, esl * 512:(esl + 1) * 512],
                            in_=pq[:, 0:512], func=AF.Relu, scale=-1.0))
                        em.op("scalar", lambda E, pq=pq, esl=esl: E.activation(
                            out=ekb[:, esl * 512:(esl + 1) * 512],
                            in_=pq[:, 512:1024], func=AF.Relu, scale=-1.0))
                        em.op("scalar", lambda E, esl=esl: E.activation(
                            out=eqb[:, esl * 512:(esl + 1) * 512],
                            in_=eqb[:, esl * 512:(esl + 1) * 512],
                            func=AF.Exp, scale=-1.0))
                        eq_abs[(st_a, sm1)] = em.op(
                            "scalar", lambda E, esl=esl: E.activation(
                                out=ekb[:, esl * 512:(esl + 1) * 512],
                                in_=ekb[:, esl * 512:(esl + 1) * 512],
                                func=AF.Exp, scale=-1.0), inc=se)

                    d1 = 0 <= sm1 < NSTEP
                    d2 = 0 <= sm2 < NSTEP
                    d3 = 0 <= sm3 < NSTEP

                    def vpad():
                        em.op("vector", lambda E: E.memset(padt[:, :], 0.0))

                    if d1:
                        h1, _ = divmod(sm1, 4)
                        pq1 = pbig0 if sm1 % 2 == 0 else pbig1
                        esl1 = sm1 % 2
                        q41 = sm1 % 4
                        em.wait("vector", se, eq_abs[(st_a, sm1)])
                        em.wait("vector", pe, qcnt[(st_a, sm1)][2])
                        em.op("vector", lambda E, pq=pq1, esl=esl1, q4=q41:
                              E.scalar_tensor_tensor(
                                  phiq[:, q4 * 512:(q4 + 1) * 512],
                                  pq[:, 0:512], 0.0,
                                  eqb[:, esl * 512:(esl + 1) * 512],
                                  OP.max, OP.add))
                        em.op("vector", lambda E, pq=pq1, esl=esl1:
                              E.scalar_tensor_tensor(
                                  phik[:, esl * 512:(esl + 1) * 512],
                                  pq[:, 512:1024], 0.0,
                                  ekb[:, esl * 512:(esl + 1) * 512],
                                  OP.max, OP.add))
                    if d2:
                        esl2 = sm2 % 2
                        em.wait("vector", pe, dencnt[(st_a, sm2)])
                        prev_rbmm = (st_a, sm2 - 2) if sm2 - 2 >= 0 else \
                            ((st_a - 1, NSTEP - 2 + sm2) if st_a > 0 else None)
                        if prev_rbmm in rbcnt:
                            em.wait("vector", pe, rbcnt[prev_rbmm])
                        em.op("vector", lambda E, esl=esl2: E.tensor_scalar(
                            dencb[0:1, esl * 512:(esl + 1) * 512],
                            pbig0[0:1, 1536:2048], EPS_DEN, None, OP.max))
                    elif d1:
                        vpad()
                    if d1:
                        kv_abs[(st_a, sm1)] = em.op(
                            "vector", lambda E, pq=pq1, esl=esl1:
                            E.tensor_tensor(
                                kvb[:, esl * 512:(esl + 1) * 512],
                                phik[:, esl * 512:(esl + 1) * 512],
                                pq[:, 1024:1536], OP.mult), inc=ve)
                    elif d2:
                        vpad()
                    if d2:
                        denc_abs[(st_a, sm2)] = em.op(
                            "vector", lambda E, esl=esl2: E.reciprocal(
                                dencb[0:1, esl * 512:(esl + 1) * 512],
                                dencb[0:1, esl * 512:(esl + 1) * 512]), inc=ve)
                    elif d1:
                        vpad()
                    if d1:
                        kvs_cnt = em.op("vector", lambda E, esl=esl1, q4=q41,
                                        h=h1: E.tensor_tensor_scan(
                                            kvs[:, q4 * 512:(q4 + 1) * 512],
                                            dect[:, h * TTILE:(h + 1) * TTILE],
                                            kvb[:, esl * 512:(esl + 1) * 512],
                                            kvc[:, h:h + 1], OP.mult, OP.add),
                                        inc=ve)
                    elif d2:
                        vpad()
                    if d2:
                        rb16_abs[(st_a, sm2)] = em.op(
                            "vector", lambda E, esl=esl2: E.tensor_copy(
                                rb16[0:1, esl * 512:(esl + 1) * 512],
                                dencb[0:1, esl * 512:(esl + 1) * 512]), inc=ve)
                    elif d1:
                        vpad()
                    if d1:
                        em.wait("vector", ve, kvs_cnt)
                        em.op("vector", lambda E, q4=q41, h=h1: E.tensor_copy(
                            kvc[:, h:h + 1],
                            kvs[:, q4 * 512 + 511:q4 * 512 + 512]))
                    if d3:
                        h3_, tl3 = divmod(sm3, 4)
                        q43 = sm3 % 4
                        osl = sm3 % 4
                        em.wait("vector", pe, rbcnt[(st_a, sm3)])
                        prev_o = (st_a, sm3 - 4) if sm3 - 4 >= 0 else \
                            ((st_a - 1, NSTEP - 4 + sm3) if st_a > 0 else None)
                        if prev_o in odma_abs:
                            em.wait("vector", st, odma_abs[prev_o])
                        em.op("vector", lambda E, q4=q43: E.tensor_tensor(
                            t0b[:, :], phiq[:, q4 * 512:(q4 + 1) * 512],
                            kvs[:, q4 * 512:(q4 + 1) * 512], OP.mult))
                    if d1:
                        kss_cnt = em.op("vector", lambda E, esl=esl1, h=h1:
                                        E.tensor_tensor_scan(
                                            kss[:, esl * 512:(esl + 1) * 512],
                                            dect[:, h * TTILE:(h + 1) * TTILE],
                                            phik[:, esl * 512:(esl + 1) * 512],
                                            kcc[:, h:h + 1], OP.mult, OP.add),
                                        inc=ve)
                    elif d3:
                        vpad()
                    if not d3 and d1:
                        vpad()
                    if d3:
                        omul_abs[(st_a, sm3)] = em.op(
                            "vector", lambda E, osl=osl: E.tensor_tensor(
                                obuf[:, osl * 512:(osl + 1) * 512],
                                t0b[:, :], pbig1[:, 1536:2048], OP.mult),
                            inc=ve)
                        em.wait("sync", ve, omul_abs[(st_a, sm3)])
                        gcol = st_a * TS + tl3 * 512
                        odma_abs[(st_a, sm3)] = em.op(
                            "sync", lambda E, h=h3_, gcol=gcol, osl=osl:
                            E.dma_start(
                                out=o_mine[h * 128:(h + 1) * 128,
                                           gcol:gcol + 512],
                                in_=obuf[:, osl * 512:(osl + 1) * 512]),
                            inc=st, amt=16)
                    if d1:
                        em.wait("vector", ve, kss_cnt)
                        em.op("vector", lambda E, esl=esl1, h=h1: E.tensor_copy(
                            kcc[:, h:h + 1],
                            kss[:, esl * 512 + 511:esl * 512 + 512]))
                        prev_prod = (st_a, sm1 - 2) if sm1 - 2 >= 0 else \
                            ((st_a - 1, NSTEP - 2 + sm1) if st_a > 0 else None)
                        if prev_prod in dencnt:
                            em.wait("vector", pe, dencnt[prev_prod])
                        prod_abs[(st_a, sm1)] = em.op(
                            "vector", lambda E, esl=esl1, q4=q41:
                            E.tensor_tensor(
                                prodb[:, esl * 512:(esl + 1) * 512],
                                phiq[:, q4 * 512:(q4 + 1) * 512],
                                kss[:, esl * 512:(esl + 1) * 512],
                                OP.mult), inc=ve)
                    if d1:
                        em.wait("tensor", ve, prod_abs[(st_a, sm1)])
                        prev_den = (st_a, sm1 - 1) if sm1 - 1 >= 0 else \
                            ((st_a - 1, NSTEP - 1) if st_a > 0 else None)
                        if prev_den in denc_abs:
                            em.wait("tensor", ve, denc_abs[prev_den])
                        dencnt[(st_a, sm1)] = em.op(
                            "tensor", lambda E, sl=sm1 % 2: E.matmul(
                                pbig0[0:1, 1536:2048], onec[:, :],
                                prodb[:, sl * 512:(sl + 1) * 512],
                                start=True, stop=True), inc=pe)
                    if 0 <= sm2 < NSTEP:
                        em.wait("tensor", ve, rb16_abs[(st_a, sm2)])
                        if sm2 == 0:
                            em.wait("tensor", ve, cp_done_tile[(st_a, 3)])
                        prev_rb = (st_a, sm2 - 1) if sm2 - 1 >= 0 else \
                            ((st_a - 1, NSTEP - 1) if st_a > 0 else None)
                        if prev_rb in omul_abs:
                            em.wait("tensor", ve, omul_abs[prev_rb])
                        rbcnt[(st_a, sm2)] = em.op(
                            "tensor", lambda E, sl=sm2 % 2: E.matmul(
                                pbig1[:, 1536:2048], oner[:, :],
                                rb16[0:1, sl * 512:(sl + 1) * 512],
                                start=True, stop=True), inc=pe)

                    if s < NSTEP and s % 4 == 3:
                        ghi = st_a * HLOC + s // 4
                        if ghi + 2 < NSTA * HLOC:
                            em.wait("gpsimd", pe, qcnt[(st_a, s)][2])
                            c_wh[ghi + 2] = em.op(
                                "gpsimd", lambda E, g=ghi + 2: wh_dma(E, g),
                                inc=wd, amt=16)

            for st_a in range(NSTA):
                if st_a > 0:
                    em.wait("vector", pe, qcnt[(st_a - 1, NSTEP - 1)][2])
                for tl in range(4):
                    plan_prep_tile(st_a, tl)
                plan_heads(st_a)

            st_total_A = st.n
            if debug:
                em.wait("gpsimd", st, st_total_A)
                em.op("gpsimd", lambda E: E.dma_start(
                    out=dbg["omine"][:, :], in_=o_mine[:, :]), inc=wd, amt=16)
            with nc.Block("blkA") as blkA:
                em.register(blkA)
        if "C" not in phases:
            return nc

        with ExitStack() as SC:
            ec = SC.enter_context
            woutb = ec(nc.sbuf_tensor("C_wout", [128, NKC_C * D], BF16))
            otb = ec(nc.sbuf_tensor("C_o", [128, 2 * NKC_C * 512], BF16))
            pstg = ec(nc.sbuf_tensor("C_pstg", [128, 2 * D], BF16))
            afb = ec(nc.sbuf_tensor("C_af", [128, 2 * D], BF16))
            xcb = ec(nc.sbuf_tensor("C_xc", [128, 2 * D], F32))
            t1c = ec(nc.sbuf_tensor("C_t1", [128, D], F32))
            xmb = ec(nc.sbuf_tensor("C_xm", [128, 2 * D], F32))
            statc = ec(nc.sbuf_tensor("C_stat", [128, 12], F32))
            cbig0 = ec(nc.psum_tensor("C_p0", [128, 2048], F32))
            cbig1 = ec(nc.psum_tensor("C_p1", [128, 2048], F32))

            em = Emitter()

            em.wait("gpsimd", st, st_total_A)
            c_wout = em.op("gpsimd", lambda E: E.dma_start(
                out=woutb[:, :], in_=wout_h[:, :]), inc=wd, amt=16)

            em.wait("sync", st, st_total_A)

            def ot_dma(E, tile):
                slot = tile % 2
                goff = tile * 512
                src = o_mine[:, :].rearrange("(kc p) t -> p kc t", p=128)[
                    :, :, goff:goff + 512]
                dstv = otb[:, slot * NKC_C * 512:(slot + 1) * NKC_C * 512]
                dst = dstv.rearrange("p (kc t) -> p kc t", kc=NKC_C)
                return E.dma_start(out=dst, in_=src)

            c_ot = {0: em.op("sync", lambda E: ot_dma(E, 0), inc=xl, amt=16),
                    1: em.op("sync", lambda E: ot_dma(E, 1), inc=xl, amt=16)}
            mm_cnt = {}
            evac_cnt = {}
            pdma_cnt = {}
            em.wait("tensor", wd, c_wout)
            for tile in range(16):
                slot = tile % 2
                em.wait("tensor", xl, c_ot[tile])
                for tc in range(4):
                    tcg = tile * 4 + tc
                    cp = cbig0 if tcg % 2 == 0 else cbig1
                    if tcg - 2 >= 0:
                        em.wait("tensor", ve, evac_cnt[tcg - 2])
                    for dt in range(4):
                        for kc in range(NKC_C):
                            f = (lambda E, cp=cp, dt=dt, kc=kc, slot=slot,
                                 tc=tc: E.matmul(
                                     cp[:, dt * 512:(dt + 1) * 512],
                                     otb[:, slot * NKC_C * 512 + kc * 512 +
                                         tc * 128:slot * NKC_C * 512 +
                                         kc * 512 + tc * 128 + 128],
                                     woutb[:, kc * D + dt * 512:
                                           kc * D + (dt + 1) * 512],
                                     start=(kc == 0), stop=(kc == NKC_C - 1)))
                            if dt == 3 and kc == NKC_C - 1:
                                mm_cnt[tcg] = em.op("tensor", f, inc=pe)
                            else:
                                em.op("tensor", f)
                    em.wait("vector", pe, mm_cnt[tcg])
                    if tcg - 2 >= 0:
                        em.wait("vector", st, pdma_cnt[tcg - 2])
                    evac_cnt[tcg] = em.op(
                        "vector", lambda E, cp=cp, sl=tcg % 2: E.tensor_copy(
                            pstg[:, sl * D:(sl + 1) * D], cp[:, :]), inc=ve)
                    em.wait("sync", ve, evac_cnt[tcg])
                    pdma_cnt[tcg] = em.op(
                        "sync", lambda E, tcg=tcg, sl=tcg % 2: E.dma_start(
                            out=partial_h[tcg * 128:(tcg + 1) * 128, :],
                            in_=pstg[:, sl * D:(sl + 1) * D]), inc=st, amt=16)
                if tile + 2 < 16:
                    em.wait("sync", pe, mm_cnt[tile * 4 + 3])
                    c_ot[tile + 2] = em.op(
                        "sync", lambda E, t=tile + 2: ot_dma(E, t),
                        inc=xl, amt=16)

            em.wait("gpsimd", st, pdma_cnt[63])
            c_cc = em.op("gpsimd", lambda E: E.collective_compute(
                "ReduceScatter", OP.add,
                replica_groups=[[0, 1], [2, 3], [4, 5], [6, 7]],
                ins=[partial_h[:, :]], outs=[attnf_h[:, :]]), inc=cc)
            if debug:
                em.wait("gpsimd", cc, c_cc)
                em.op("gpsimd", lambda E: E.dma_start(
                    out=dbg["attnf"][:, :], in_=attnf_h[:, :]), inc=wd, amt=16)

            em.wait("sync", cc, c_cc)

            def af_dma(E, ch):
                sl = ch % 2
                return E.dma_start(out=afb[:, sl * D:(sl + 1) * D],
                                   in_=attnf_h[ch * 128:(ch + 1) * 128, :])

            def xc_dma(E, ch):
                sl = ch % 2
                return E.dma_start(out=xcb[:, sl * D:(sl + 1) * D],
                                   in_=xc_h[ch * 128:(ch + 1) * 128, :])

            c_af = {0: em.op("sync", lambda E: af_dma(E, 0), inc=xl, amt=16),
                    1: em.op("sync", lambda E: af_dma(E, 1), inc=xl, amt=16)}
            c_xc = {0: em.op("sync", lambda E: xc_dma(E, 0), inc=xl, amt=16),
                    1: em.op("sync", lambda E: xc_dma(E, 1), inc=xl, amt=16)}
            xm_dma = {}
            add_cnt = {}
            for ch in range(32):
                sl = ch % 2
                X = afb[:, sl * D:(sl + 1) * D]
                pre = [("vector", xl, c_af[ch]), ("scalar", xl, c_af[ch])]
                if ch - 1 in add_cnt:
                    pre.append(("scalar", ve, add_cnt[ch - 1]))
                v2, s1 = ln_plan(em, X, t1c[:, :], t1c[:, :], statc,
                                 (ch % 2) * 6, pre)
                em.wait("vector", xl, c_xc[ch])
                em.wait("vector", ve, v2)
                if ch - 2 in xm_dma:
                    em.wait("vector", st, xm_dma[ch - 2])
                add_cnt[ch] = em.op(
                    "vector", lambda E, sl=sl: E.tensor_tensor(
                        xmb[:, sl * D:(sl + 1) * D], t1c[:, :],
                        xcb[:, sl * D:(sl + 1) * D], OP.add), inc=ve)
                em.wait("sync", ve, add_cnt[ch])
                xm_dma[ch] = em.op(
                    "sync", lambda E, ch=ch, sl=sl: E.dma_start(
                        out=xmid_h[ch * 128:(ch + 1) * 128, :],
                        in_=xmb[:, sl * D:(sl + 1) * D]), inc=st, amt=16)
                if ch + 2 < 32:
                    em.wait("sync", ve, v2)
                    em.wait("sync", se, s1)
                    c_af[ch + 2] = em.op(
                        "sync", lambda E, c2=ch + 2: af_dma(E, c2),
                        inc=xl, amt=16)
                    em.wait("sync", ve, add_cnt[ch])
                    c_xc[ch + 2] = em.op(
                        "sync", lambda E, c2=ch + 2: xc_dma(E, c2),
                        inc=xl, amt=16)
            xm_last = xm_dma[31]
            c1_pe_last = mm_cnt[63]
            c1_ve_last = evac_cnt[63]
            if debug:
                em.wait("gpsimd", st, xm_last)
                em.op("gpsimd", lambda E: E.dma_start(
                    out=dbg["xmid"][:, :], in_=xmid_h[:, :]), inc=wd, amt=16)
            with nc.Block("blkC") as blkC:
                em.register(blkC)
        if "D" not in phases:
            return nc

        with ExitStack() as SD:
            ed = SD.enter_context
            gateb = ed(nc.sbuf_tensor("D_gw", [128, NKC * 16], BF16))
            ln2fm = ed(nc.sbuf_tensor("D_ln2", [128, NKC * TS], BF16))
            hb = ed(nc.sbuf_tensor("D_h", [128, NFC * TS], BF16))
            wffb = ed(nc.sbuf_tensor("D_wff", [128, 2 * NKC * 2 * 128], BF16))
            w2b = ed(nc.sbuf_tensor("D_w2", [128, 4 * NFC * 512], BF16))
            ytmp = ed(nc.sbuf_tensor("D_y", [128, 2 * D], F32))
            lnt_b = ytmp[:, 0:D].bitcast(BF16)[:, 0:D]
            lnt = ed(nc.sbuf_tensor("D_lnt", [128, D], F32))
            xmb2 = ed(nc.sbuf_tensor("D_xm", [128, D], F32))
            silu_sb = ed(nc.sbuf_tensor("D_silu", [128, 512], F32))
            gates = ed(nc.sbuf_tensor("D_g", [128, 16 * NE], F32))
            gsc = ed(nc.sbuf_tensor("D_gs", [128, 16], F32))
            gsc2 = ed(nc.sbuf_tensor("D_gs2", [128, 16], F32))
            statd = ed(nc.sbuf_tensor("D_stat", [128, 12], F32))
            dbig0 = ed(nc.psum_tensor("D_p0", [128, 2048], F32))
            dbig1 = ed(nc.psum_tensor("D_p1", [128, 2048], F32))
            tpDV = [dbig1[:, k * 512:(k + 1) * 512].bitcast(BF16)
                    for k in range(4)]

            em = Emitter()

            em.wait("gpsimd", st, xm_last)
            em.wait("gpsimd", pe, c1_pe_last)
            c_gw = em.op("gpsimd", lambda E: E.dma_start(
                out=gateb[:, :], in_=gatew_h[:, :]), inc=wd, amt=16)

            NW = NST_D * NE * NFC

            def wff_dma(E, g):
                ste, fc = divmod(g, NFC)
                e = ste % NE
                slot = g % 2
                return E.dma_start(
                    out=wffb[:, slot * NKC * 256:(slot + 1) * NKC * 256],
                    in_=wff_hs[e][fc, :, :])

            def w2_dma(E, ge):
                e = ge % NE
                return E.dma_start(out=w2b[:, :], in_=w2t_hs[e][:, :])

            c_wffc = {0: em.op("gpsimd", lambda E: wff_dma(E, 0), inc=wd, amt=16),
                      1: em.op("gpsimd", lambda E: wff_dma(E, 1), inc=wd, amt=16)}
            c_w2c = {0: em.op("gpsimd", lambda E: w2_dma(E, 0), inc=wd, amt=16)}

            em.wait("sync", st, xm_last)

            hcnt = {}
            silucnt = {}
            hevcnt = {}
            ymcnt = {}
            yevcnt = {}
            ydma = {}
            ylast_pe = {}
            tpd = {}
            cpd = {}
            gate_ve = {}
            gate_se = {}
            gate_nmx = {}
            c_xm2 = {}
            c_yl = {}
            c_xm3 = {}
            outdma = {}
            addc = {}

            def xm2_dma(E, gch):
                return E.dma_start(out=xmb2[:, :],
                                   in_=xmid_h[gch * 128:(gch + 1) * 128, :])

            prev_norm = {}

            for st_d in range(NST_D):
                for ch in range(16):
                    gch = st_d * 16 + ch
                    if ch == 0:
                        if st_d > 0:
                            em.wait("sync", ve, addc[(st_d - 1, 15)])
                        c_xm2[gch] = em.op(
                            "sync", lambda E, g=gch: xm2_dma(E, g),
                            inc=xl, amt=16)
                    pre = [("vector", xl, c_xm2[gch]),
                           ("scalar", xl, c_xm2[gch])]
                    if (gch - 1, NKC - 1) in tpd:
                        pre.append(("scalar", pe, tpd[(gch - 1, NKC - 1)]))
                    elif gch == 0:
                        pre.append(("scalar", ve, c1_ve_last))
                    if ch == 0 and st_d > 0:
                        pre.append(("scalar", st, outdma[(st_d - 1, 15)]))
                        pre.append(("vector", st, outdma[(st_d - 1, 15)]))
                    v2, s1 = ln_plan(em, xmb2[:, :], lnt_b[:, :], lnt_b[:, :],
                                     statd, (gch % 2) * 6, pre)
                    prev_norm[gch] = (v2, s1)
                    if ch + 1 < 16:
                        em.wait("sync", ve, v2)
                        em.wait("sync", se, s1)
                        c_xm2[gch + 1] = em.op(
                            "sync", lambda E, g=gch + 1: xm2_dma(E, g),
                            inc=xl, amt=16)
                    em.wait("tensor", ve, v2)
                    if ch == 0:
                        if st_d == 0:
                            em.wait("tensor", ve, c1_ve_last)
                        else:
                            em.wait("tensor", ve,
                                    hevcnt[(st_d * NE - 1, NFC - 1, 3)])
                        if st_d > 0:
                            em.wait("vector", pe,
                                    hcnt[(st_d * NE - 1, NFC - 1, 3)])
                    for kc in range(NKC):
                        prevk = (gch, kc - 4) if kc >= 4 else \
                            (gch - 1, NKC - 4 + kc)
                        if prevk in cpd:
                            em.wait("tensor", ve, cpd[prevk])
                        tpv = tpDV[kc % 4]
                        f = (lambda E, kc=kc, tpv=tpv: E.transpose(
                            tpv[:, 0:128],
                            lnt_b[:, kc * 128:(kc + 1) * 128], ident[:, :]))
                        tpd[(gch, kc)] = em.op("tensor", f, inc=pe)
                        em.wait("vector", pe, tpd[(gch, kc)])
                        dst = ln2fm[:, kc * TS + ch * 128:
                                    kc * TS + ch * 128 + 128]
                        cpd[(gch, kc)] = em.op(
                            "vector", lambda E, tpv=tpv, dst=dst:
                            E.tensor_copy(dst, tpv[:, 0:128]), inc=ve)
                    em.wait("tensor", ve, cpd[(gch, NKC - 1)])
                    em.wait("tensor", wd, c_gw)
                    if ch == 0:
                        if st_d == 0:
                            em.wait("tensor", ve, c1_ve_last)
                        else:
                            em.wait("tensor", ve,
                                    yevcnt[(st_d * NE - 1, 62)])
                            em.wait("tensor", ve,
                                    yevcnt[(st_d * NE - 1, 63)])
                            em.wait("tensor", se,
                                    silucnt[(st_d * NE - 1, NFC - 1, 2)])
                            em.wait("tensor", se,
                                    silucnt[(st_d * NE - 1, NFC - 1, 3)])
                    else:
                        em.wait("tensor", ve, gate_nmx[gch - 1])
                        em.wait("tensor", se, gate_se[gch - 1])
                    for kc in range(NKC):
                        f = (lambda E, kc=kc, ch=ch: E.matmul(
                            dbig0[:, 0:16],
                            ln2fm[:, kc * TS + ch * 128:kc * TS + ch * 128 + 128],
                            gateb[:, kc * 16:(kc + 1) * 16],
                            start=(kc == 0), stop=(kc == NKC - 1)))
                        if kc == NKC - 1:
                            g_pe = em.op("tensor", f, inc=pe)
                        else:
                            em.op("tensor", f)
                    em.wait("vector", pe, g_pe)
                    gate_nmx[gch] = em.op("vector", lambda E: E.tensor_reduce(
                        statd[:, 10:11], dbig0[:, 0:16], AX.X, OP.max,
                        negate=True), inc=ve)
                    em.wait("scalar", ve, gate_nmx[gch])
                    em.wait("scalar", pe, g_pe)
                    gate_se[gch] = em.op("scalar", lambda E: E.activation(
                        out=gsc[:, :], in_=dbig0[:, 0:16], func=AF.Exp,
                        bias=statd[:, 10:11]), inc=se)
                    em.wait("vector", se, gate_se[gch])
                    em.op("vector", lambda E: E.tensor_reduce(
                        statd[:, 11:12], gsc[:, :], AX.X, OP.add))
                    c = em.op("vector", lambda E: E.tensor_reduce(
                        statd[:, 10:11], gsc[:, :], AX.X, OP.max), inc=ve)
                    em.wait("vector", ve, c)
                    em.op("vector", lambda E: E.tensor_scalar(
                        gsc2[:, :], gsc[:, :], statd[:, 10:11], None,
                        OP.is_equal))
                    em.op("vector", lambda E: E.reciprocal(
                        statd[:, 11:12], statd[:, 11:12]))
                    c = em.op("vector", lambda E: E.scalar_tensor_tensor(
                        gsc2[:, :], gsc2[:, :], -BIG, gsc[:, :],
                        OP.mult, OP.add), inc=ve)
                    em.wait("vector", ve, c)
                    c = em.op("vector", lambda E: E.tensor_reduce(
                        statd[:, 10:11], gsc2[:, :], AX.X, OP.max), inc=ve)
                    em.op("vector", lambda E, ch=ch: E.memset(
                        gates[:, ch * NE + 16:ch * NE + 18], 1.0))
                    em.wait("vector", ve, c)
                    c = em.op("vector", lambda E: E.tensor_scalar(
                        gsc2[:, :], gsc[:, :], statd[:, 10:11], None,
                        OP.is_ge), inc=ve)
                    c2 = em.op("vector", lambda E: E.tensor_scalar(
                        gsc[:, :], gsc[:, :], statd[:, 11:12], None, OP.mult),
                        inc=ve)
                    em.wait("vector", ve, c)
                    em.wait("vector", ve, c2)
                    gate_ve[gch] = em.op(
                        "vector", lambda E, ch=ch: E.tensor_tensor(
                            gates[:, ch * NE:ch * NE + 16], gsc[:, :],
                            gsc2[:, :], OP.mult), inc=ve)

                for e in range(NE):
                    ste = st_d * NE + e
                    for fc in range(NFC):
                        g = ste * NFC + fc
                        em.wait("tensor", wd, c_wffc[g])
                        wsl = g % 2
                        for tt in range(4):
                            if fc == 0:
                                if e == 0:
                                    if tt == 0:
                                        em.wait("tensor", ve,
                                                gate_ve[st_d * 16 + 15])
                                        em.wait("tensor", se,
                                                gate_se[st_d * 16 + 15])
                                    if st_d > 0:
                                        psl = st_d * NE - 1
                                        if tt <= 1:
                                            em.wait("tensor", ve,
                                                    yevcnt[(psl, 62 + tt)])
                                        em.wait("tensor", se,
                                                silucnt[(psl, NFC - 1, tt)])
                                        em.wait("tensor", ve,
                                                hevcnt[(psl, NFC - 1, tt)])
                                    em.wait("tensor", ve,
                                            cpd[(st_d * 16 + 15, NKC - 1)])
                                else:
                                    if tt <= 1:
                                        em.wait("tensor", ve,
                                                yevcnt[(ste - 1, 62 + tt)])
                                    em.wait("tensor", se,
                                            silucnt[(ste - 1, NFC - 1, tt)])
                                    em.wait("tensor", ve,
                                            hevcnt[(ste - 1, NFC - 1, tt)])
                            else:
                                em.wait("tensor", se,
                                        silucnt[(ste, fc - 1, tt)])
                                em.wait("tensor", ve,
                                        hevcnt[(ste, fc - 1, tt)])
                            for kc in range(NKC):
                                f1 = (lambda E, tt=tt, kc=kc, wsl=wsl:
                                      E.matmul(
                                          dbig0[:, tt * 512:(tt + 1) * 512],
                                          wffb[:, wsl * NKC * 256 +
                                               kc * 256:wsl * NKC * 256 +
                                               kc * 256 + 128],
                                          ln2fm[:, kc * TS + tt * 512:
                                                kc * TS + (tt + 1) * 512],
                                          start=(kc == 0),
                                          stop=(kc == NKC - 1)))
                                em.op("tensor", f1)
                                f3 = (lambda E, tt=tt, kc=kc, wsl=wsl:
                                      E.matmul(
                                          dbig1[:, tt * 512:(tt + 1) * 512],
                                          wffb[:, wsl * NKC * 256 +
                                               kc * 256 + 128:
                                               wsl * NKC * 256 + kc * 256 + 256],
                                          ln2fm[:, kc * TS + tt * 512:
                                                kc * TS + (tt + 1) * 512],
                                          start=(kc == 0),
                                          stop=(kc == NKC - 1)))
                                if kc == NKC - 1:
                                    hcnt[(ste, fc, tt)] = em.op(
                                        "tensor", f3, inc=pe)
                                else:
                                    em.op("tensor", f3)
                            sid = (fc * 4 + tt)
                            ssl = 0
                            em.wait("scalar", pe, hcnt[(ste, fc, tt)])
                            psid = sid - 1
                            if psid >= 0:
                                em.wait("scalar", ve,
                                        hevcnt[(ste, psid // 4, psid % 4)])
                            elif ste > 0:
                                em.wait("scalar", ve,
                                        hevcnt[(ste - 1, NFC - 1, 3)])
                            silucnt[(ste, fc, tt)] = em.op(
                                "scalar", lambda E, tt=tt, ssl=ssl:
                                E.activation(
                                    out=silu_sb[:, ssl * 512:(ssl + 1) * 512],
                                    in_=dbig0[:, tt * 512:(tt + 1) * 512],
                                    func=AF.Silu), inc=se)
                            em.wait("vector", se, silucnt[(ste, fc, tt)])
                            if fc == 0 and tt == 0 and ste > 0:
                                em.wait("vector", pe, ylast_pe[ste - 1])
                            hevcnt[(ste, fc, tt)] = em.op(
                                "vector", lambda E, fc=fc, tt=tt, ssl=ssl:
                                E.tensor_tensor(
                                    hb[:, fc * TS + tt * 512:
                                       fc * TS + (tt + 1) * 512],
                                    silu_sb[:, ssl * 512:(ssl + 1) * 512],
                                    dbig1[:, tt * 512:(tt + 1) * 512],
                                    OP.mult), inc=ve)
                        if g + 2 < NW:
                            em.wait("gpsimd", pe, hcnt[(ste, fc, 3)])
                            c_wffc[g + 2] = em.op(
                                "gpsimd", lambda E, g2=g + 2: wff_dma(E, g2),
                                inc=wd, amt=16)
                    em.wait("tensor", wd, c_w2c[ste])
                    for tc in range(16):
                        for dt in range(4):
                            idx = tc * 4 + dt
                            sl = idx % 2
                            if idx == 0:
                                em.wait("tensor", ve,
                                        hevcnt[(ste, NFC - 1, 3)])
                            if idx <= 1:
                                em.wait("tensor", se,
                                        silucnt[(ste, NFC - 1, idx)])
                            if idx - 2 >= 0:
                                em.wait("tensor", ve, yevcnt[(ste, idx - 2)])
                            for fc in range(NFC):
                                f = (lambda E, sl=sl, fc=fc, tc=tc, dt=dt:
                                     E.matmul(
                                         dbig0[:, sl * 512:(sl + 1) * 512],
                                         hb[:, fc * TS + tc * 128:
                                            fc * TS + tc * 128 + 128],
                                         w2b[:, (dt * NFC + fc) * 512:
                                             (dt * NFC + fc + 1) * 512],
                                         start=(fc == 0),
                                         stop=(fc == NFC - 1)))
                                if fc == NFC - 1:
                                    ymcnt[(ste, idx)] = em.op(
                                        "tensor", f, inc=pe)
                                else:
                                    em.op("tensor", f)
                            em.wait("vector", pe, ymcnt[(ste, idx)])
                            if dt == 0 and (ste, tc - 2) in ydma:
                                em.wait("vector", wd, ydma[(ste, tc - 2)])
                            if dt == 0 and tc <= 1 and ste > 0:
                                if e > 0:
                                    em.wait("vector", wd,
                                            ydma[(ste - 1, 14 + tc)])
                                else:
                                    em.wait("vector", st,
                                            outdma[(st_d - 1, 14 + tc)])
                            yevcnt[(ste, idx)] = em.op(
                                "vector", lambda E, sl=sl, tc=tc, dt=dt, e=e:
                                E.tensor_scalar(
                                    ytmp[:, (tc % 2) * D + dt * 512:
                                         (tc % 2) * D + (dt + 1) * 512],
                                    dbig0[:, sl * 512:(sl + 1) * 512],
                                    gates[:, tc * NE + e:tc * NE + e + 1],
                                    None, OP.mult), inc=ve)
                        em.wait("gpsimd", ve, yevcnt[(ste, tc * 4 + 3)])
                        if e == 0:
                            f = (lambda E, st_d=st_d, tc=tc: E.dma_start(
                                out=y_h[st_d, tc * 128:(tc + 1) * 128, :],
                                in_=ytmp[:, (tc % 2) * D:(tc % 2 + 1) * D]))
                        else:
                            f = (lambda E, st_d=st_d, tc=tc: E.dma_start(
                                out=y_h[st_d, tc * 128:(tc + 1) * 128, :],
                                in_=ytmp[:, (tc % 2) * D:(tc % 2 + 1) * D],
                                accum_op=OP.add))
                        ydma[(ste, tc)] = em.op("gpsimd", f, inc=wd, amt=16)
                    ylast_pe[ste] = ymcnt[(ste, 63)]
                    if ste + 1 < NST_D * NE:
                        em.wait("gpsimd", pe, ylast_pe[ste])
                        c_w2c[ste + 1] = em.op(
                            "gpsimd", lambda E, g2=ste + 1: w2_dma(E, g2),
                            inc=wd, amt=16)

                y_all = ydma[(st_d * NE + NE - 1, 15)]
                em.wait("sync", wd, y_all)

                def yl_dma(E, st_d, ch):
                    sl = ch % 2
                    return E.dma_start(out=ytmp[:, sl * D:(sl + 1) * D],
                                       in_=y_h[st_d, ch * 128:(ch + 1) * 128, :])

                def xm3_dma(E, st_d, ch):
                    gr = st_d * TS + ch * 128
                    return E.dma_start(out=xmb2[:, :],
                                       in_=xmid_h[gr:gr + 128, :])

                c_yl[(st_d, 0)] = em.op(
                    "sync", lambda E, sd=st_d: yl_dma(E, sd, 0),
                    inc=xl, amt=16)
                c_yl[(st_d, 1)] = em.op(
                    "sync", lambda E, sd=st_d: yl_dma(E, sd, 1),
                    inc=xl, amt=16)
                v2p, s1p = prev_norm[st_d * 16 + 15]
                em.wait("sync", ve, v2p)
                em.wait("sync", se, s1p)
                c_xm3[(st_d, 0)] = em.op(
                    "sync", lambda E, sd=st_d: xm3_dma(E, sd, 0),
                    inc=xl, amt=16)
                for ch in range(16):
                    sl = ch % 2
                    X = ytmp[:, sl * D:(sl + 1) * D]
                    pre = [("vector", xl, c_yl[(st_d, ch)]),
                           ("scalar", xl, c_yl[(st_d, ch)])]
                    if ch == 0:
                        if st_d > 0:
                            pre.append(("scalar", ve, addc[(st_d - 1, 15)]))
                    else:
                        pre.append(("scalar", ve, addc[(st_d, ch - 1)]))
                    v2, s1 = ln_plan(em, X, lnt[:, :], lnt[:, :], statd,
                                     (ch % 2) * 6, pre)
                    em.wait("vector", xl, c_xm3[(st_d, ch)])
                    em.wait("vector", ve, v2)
                    addc[(st_d, ch)] = em.op(
                        "vector", lambda E, sl=sl: E.tensor_tensor(
                            ytmp[:, sl * D:(sl + 1) * D], lnt[:, :],
                            xmb2[:, :], OP.add), inc=ve)
                    em.wait("sync", ve, addc[(st_d, ch)])
                    gout = st_d * TS + ch * 128
                    outdma[(st_d, ch)] = em.op(
                        "sync", lambda E, gout=gout, sl=sl: E.dma_start(
                            out=out_h[gout:gout + 128, :],
                            in_=ytmp[:, sl * D:(sl + 1) * D]), inc=st, amt=16)
                    if ch + 2 < 16:
                        em.wait("sync", st, outdma[(st_d, ch)])
                        c_yl[(st_d, ch + 2)] = em.op(
                            "sync", lambda E, sd=st_d, c2=ch + 2:
                            yl_dma(E, sd, c2), inc=xl, amt=16)
                    if ch + 1 < 16:
                        em.wait("sync", ve, addc[(st_d, ch)])
                        c_xm3[(st_d, ch + 1)] = em.op(
                            "sync", lambda E, sd=st_d, c2=ch + 1:
                            xm3_dma(E, sd, c2), inc=xl, amt=16)

            if debug:
                em.wait("gpsimd", st, outdma[(NST_D - 1, 15)])
                em.op("gpsimd", lambda E: E.dma_start(
                    out=dbg["y"][:, :, :], in_=y_h[:, :, :]), inc=wd, amt=16)
            em.wait("sync", st, st.n)
            em.wait("gpsimd", wd, wd.n)
            with nc.Block("blkD") as blkD:
                em.register(blkD)

    return nc


def _prep_inputs(inputs):
    x = np.asarray(inputs["L0_attn_x"])
    wqkv = np.asarray(inputs["L0_attn_w_qkv"])
    wout = np.asarray(inputs["L0_attn_w_out"])
    dp = np.asarray(inputs["L0_attn_decay_param"])
    gw = np.asarray(inputs["L0_ffn_gate_w"])
    w1 = np.asarray(inputs["L0_ffn_w1"])
    w2 = np.asarray(inputs["L0_ffn_w2"])
    w3 = np.asarray(inputs["L0_ffn_w3"])
    sw1 = np.asarray(inputs["L0_ffn_sw1"])
    sw2 = np.asarray(inputs["L0_ffn_sw2"])
    sw3 = np.asarray(inputs["L0_ffn_sw3"])

    def tile_w13(a):
        return np.ascontiguousarray(
            a.reshape(NFC, 128, NKC, 128).transpose(0, 3, 2, 1))
    wff = np.empty((NE, NFC, 128, NKC, 2, 128), dtype=np.float32)
    for e in range(16):
        wff[e, :, :, :, 0, :] = tile_w13(w1[e])
        wff[e, :, :, :, 1, :] = tile_w13(w3[e])
    wff[16, :, :, :, 0, :] = tile_w13(sw1[0:1408])
    wff[16, :, :, :, 1, :] = tile_w13(sw3[0:1408])
    wff[17, :, :, :, 0, :] = tile_w13(sw1[1408:2816])
    wff[17, :, :, :, 1, :] = tile_w13(sw3[1408:2816])
    wff = wff.reshape(NE, NFC, 128, NKC * 2 * 128)
    wff_parts = {f"wff{e}": np.ascontiguousarray(wff[e]) for e in range(NE)}

    def tile_w2(a):
        return a.reshape(4, 512, NFC, 128).transpose(3, 0, 2, 1)
    w2t = np.empty((NE, 128, 4, NFC, 512), dtype=np.float32)
    for e in range(16):
        w2t[e] = tile_w2(w2[e])
    w2t[16] = tile_w2(sw2[:, 0:1408])
    w2t[17] = tile_w2(sw2[:, 1408:2816])
    w2t = w2t.reshape(NE, 128, 4 * NFC * 512)
    w2t_parts = {f"w2t{e}": np.ascontiguousarray(w2t[e]) for e in range(NE)}

    gwt = np.ascontiguousarray(
        gw.T.reshape(NKC, 128, 16).transpose(1, 0, 2).reshape(128, NKC * 16))

    ident = np.eye(128, dtype=ml_dtypes.bfloat16)
    onec = np.ones((128, 1), dtype=ml_dtypes.bfloat16)
    oner = np.ones((1, 128), dtype=ml_dtypes.bfloat16)

    in_maps = []
    for c in range(8):
        b, p = divmod(c, 2)
        wq = np.empty((HLOC, NKC, 128, 384), dtype=np.float32)
        for h in range(HLOC):
            g = p * HLOC + h
            blk = np.concatenate(
                [wqkv[:, g * 128:(g + 1) * 128],
                 wqkv[:, 2048 + g * 128:2048 + (g + 1) * 128],
                 wqkv[:, 4096 + g * 128:4096 + (g + 1) * 128]], axis=1)
            wq[h] = blk.reshape(NKC, 128, 384)
        wq = np.ascontiguousarray(
            wq.transpose(0, 2, 1, 3).reshape(HLOC, 128, NKC * 384))
        wo = wout[p * 1024:(p + 1) * 1024]
        wo = np.ascontiguousarray(
            wo.reshape(NKC_C, 128, D).transpose(1, 0, 2).reshape(128, NKC_C * D))
        dpc = np.ascontiguousarray(
            np.tile(dp[p * HLOC:(p + 1) * HLOC][None, :], (128, 1))
        ).astype(np.float32)
        m = {
            "xa": np.ascontiguousarray(x[b]),
            "xc": np.ascontiguousarray(x[b, p * THALF:(p + 1) * THALF]),
            "wqkv": wq,
            "wout": wo,
            "gatew": gwt,
            "dp": dpc,
            "identc": ident,
            "onec": onec,
            "oner": oner,
        }
        m.update(wff_parts)
        m.update(w2t_parts)
        in_maps.append(m)
    return in_maps


def kernel(**inputs):
    in_maps = _prep_inputs(inputs)
    nc = build_nc(debug=False)
    res = run_bass_kernel_spmd(nc, in_maps, core_ids=list(range(8)))
    out = np.empty((B, T, D), dtype=np.float32)
    for c in range(8):
        b, p = divmod(c, 2)
        out[b, p * THALF:(p + 1) * THALF] = res.results[c]["out"]
    return out


# version 04
# speedup vs baseline: 1.1914x; 1.1914x over previous
import numpy as np
import ml_dtypes
import concourse.bass as bass
import concourse.mybir as mybir
from concourse.alu_op_type import AluOpType as OP
from concourse.bass_utils import run_bass_kernel_spmd
from contextlib import ExitStack

F32 = mybir.dt.float32
BF16 = mybir.dt.bfloat16
AF = mybir.ActivationFunctionType
AX = mybir.AxisListType

ENGS = ("sync", "gpsimd", "vector", "scalar", "tensor")

B, T, D = 4, 8192, 2048
HLOC = 8
NKC = 16
NKC_C = 8
TTILE = 512
NTA = T // TTILE
NSTA = 4
NSTEP = HLOC * 4
THALF = T // 2
TS = 2048
NST_D = THALF // TS
NFC = 11
NE = 18
EPS_LN = 1e-5
EPS_DEN = 1e-6
INV_D = 1.0 / D
BIG = 1.0e30


class Sem:
    def __init__(self, h):
        self.h = h
        self.n = 0


class Emitter:
    def __init__(self):
        self.q = {k: [] for k in ENGS}

    def op(self, eng, f, inc=None, amt=1):
        if inc is not None:
            inc.n += amt
            self.q[eng].append(lambda E, f=f, s=inc.h, a=amt: f(E).then_inc(s, a))
            return inc.n
        self.q[eng].append(f)
        return None

    def wait(self, eng, sem, val):
        if val is None or val <= 0:
            return
        self.q[eng].append(lambda E, s=sem.h, v=val: E.wait_ge(s, v))

    def register(self, block):
        for eng in ENGS:
            ops = self.q[eng]

            def run(E, ops=ops):
                for f in ops:
                    f(E)
            getattr(block, eng)(run)


def build_nc(debug=False, phases='ACD'):
    nc = bass.Bass("TRN2", target_bir_lowering=False)

    xa_h = nc.dram_tensor("xa", [T, D], F32, kind="ExternalInput")
    xc_h = nc.dram_tensor("xc", [THALF, D], F32, kind="ExternalInput")
    wqkv_h = nc.dram_tensor("wqkv", [HLOC, 128, NKC * 384], F32,
                            kind="ExternalInput")
    wout_h = nc.dram_tensor("wout", [128, NKC_C * D], F32, kind="ExternalInput")
    gatew_h = nc.dram_tensor("gatew", [128, NKC * 16], F32, kind="ExternalInput")
    wff_hs = [nc.dram_tensor(f"wff{e}", [NFC, 128, NKC * 2 * 128], F32,
                             kind="ExternalInput") for e in range(NE)]
    w2t_hs = [nc.dram_tensor(f"w2t{e}", [128, 4 * NFC * 512], F32,
                             kind="ExternalInput") for e in range(NE)]
    dp_h = nc.dram_tensor("dp", [128, HLOC], F32, kind="ExternalInput")
    ident_h = nc.dram_tensor("identc", [128, 128], BF16, kind="ExternalInput")
    onec_h = nc.dram_tensor("onec", [128, 1], BF16, kind="ExternalInput")
    oner_h = nc.dram_tensor("oner", [1, 128], BF16, kind="ExternalInput")

    out_h = nc.dram_tensor("out", [THALF, D], F32, kind="ExternalOutput")

    o_mine = nc.dram_tensor("o_mine", [HLOC * 128, T], BF16)
    partial_h = nc.dram_tensor("partial", [T, D], BF16)
    attnf_h = nc.dram_tensor("attnf", [THALF, D], BF16)
    xmid_h = nc.dram_tensor("xmid", [THALF, D], F32)
    y_h = nc.dram_tensor("ybuf", [NST_D, TS, D], F32)

    dbg = {}
    if debug:
        dbg["omine"] = nc.dram_tensor("dbg_omine", [HLOC * 128, T], BF16,
                                      kind="ExternalOutput")
        dbg["attnf"] = nc.dram_tensor("dbg_attnf", [THALF, D], BF16,
                                      kind="ExternalOutput")
        dbg["xmid"] = nc.dram_tensor("dbg_xmid", [THALF, D], F32,
                                     kind="ExternalOutput")
        dbg["y"] = nc.dram_tensor("dbg_y", [NST_D, TS, D], F32,
                                  kind="ExternalOutput")

    with ExitStack() as G:
        eg = G.enter_context
        sems = {k: Sem(eg(nc.semaphore(f"s_{k}")))
                for k in ("xl", "wd", "st", "pe", "ve", "se", "cc")}
        xl, wd, st = sems["xl"], sems["wd"], sems["st"]
        pe, ve, se, cc = sems["pe"], sems["ve"], sems["se"], sems["cc"]

        ident = eg(nc.sbuf_tensor("identb", [128, 128], BF16))
        onec = eg(nc.sbuf_tensor("onecb", [128, 1], BF16))
        oner = eg(nc.sbuf_tensor("onerb", [1, 128], BF16))

        def ln_plan(em, X, OUTp, J, S, so, pre_waits=()):
            for eng, sm, val in pre_waits:
                em.wait(eng, sm, val)
            c = em.op("vector", lambda E: E.tensor_reduce(
                S[:, so:so + 1], X, AX.X, OP.add), inc=ve)
            s1 = em.op("scalar", lambda E: E.activation(
                out=J, in_=X, func=AF.Square, accum_out=S[:, so + 1:so + 2]),
                inc=se)
            em.wait("vector", ve, c)
            c = em.op("vector", lambda E: E.tensor_scalar(
                S[:, so + 2:so + 3], S[:, so:so + 1], INV_D, None, OP.mult),
                inc=ve)
            em.wait("vector", ve, c)
            c = em.op("vector", lambda E: E.tensor_tensor(
                S[:, so + 3:so + 4], S[:, so + 2:so + 3], S[:, so + 2:so + 3],
                OP.mult), inc=ve)
            em.wait("vector", ve, c)
            c = em.op("vector", lambda E: E.tensor_scalar(
                S[:, so + 3:so + 4], S[:, so + 3:so + 4], EPS_LN, None,
                OP.subtract), inc=ve)
            em.wait("vector", se, s1)
            em.wait("vector", ve, c)
            v_var = em.op("vector", lambda E: E.scalar_tensor_tensor(
                S[:, so + 3:so + 4], S[:, so + 1:so + 2], INV_D,
                S[:, so + 3:so + 4], OP.mult, OP.subtract), inc=ve)
            em.wait("scalar", ve, v_var)
            c = em.op("scalar", lambda E: E.activation(
                out=S[:, so + 4:so + 5], in_=S[:, so + 3:so + 4], func=AF.Ln),
                inc=se)
            em.wait("scalar", se, c)
            s2 = em.op("scalar", lambda E: E.activation(
                out=S[:, so + 4:so + 5], in_=S[:, so + 4:so + 5], func=AF.Exp,
                scale=-0.5), inc=se)
            em.wait("vector", se, s2)
            v_norm = em.op("vector", lambda E: E.tensor_scalar(
                OUTp, X, S[:, so + 2:so + 3], S[:, so + 4:so + 5],
                OP.subtract, OP.mult), inc=ve)
            return v_norm, s1

        with ExitStack() as SA:
            ea = SA.enter_context
            dpt = ea(nc.sbuf_tensor("A_dp", [128, HLOC], F32))
            sdec = ea(nc.sbuf_tensor("A_sdec", [128, HLOC], F32))
            xbuf = ea(nc.sbuf_tensor("A_x", [128, 2 * D], F32))
            lntm = ea(nc.sbuf_tensor("A_lntm", [128, 2 * D], BF16))
            lnfm = ea(nc.sbuf_tensor("A_lnfm", [128, NKC * TS], BF16))
            whead = ea(nc.sbuf_tensor("A_wh", [128, 2 * NKC * 384], BF16))
            dect = ea(nc.sbuf_tensor("A_dect", [128, HLOC * TTILE], F32))
            kvc = ea(nc.sbuf_tensor("A_kvc", [128, HLOC], F32))
            kcc = ea(nc.sbuf_tensor("A_kcc", [128, HLOC], F32))
            stat = ea(nc.sbuf_tensor("A_stat", [128, 12], F32))
            phiq = ea(nc.sbuf_tensor("A_phiq", [128, 4 * TTILE], BF16))
            phik = ea(nc.sbuf_tensor("A_phik", [128, 2 * TTILE], BF16))
            kvb = ea(nc.sbuf_tensor("A_kv", [128, 2 * TTILE], BF16))
            kvs = ea(nc.sbuf_tensor("A_kvs", [128, 4 * TTILE], F32))
            kss = ea(nc.sbuf_tensor("A_kss", [128, 2 * TTILE], F32))
            prodb = ea(nc.sbuf_tensor("A_prod", [128, 2 * TTILE], BF16))
            eqb = ea(nc.sbuf_tensor("A_eq", [128, 2 * TTILE], F32))
            ekb = ea(nc.sbuf_tensor("A_ek", [128, 2 * TTILE], F32))
            t0b = ea(nc.sbuf_tensor("A_t0", [128, TTILE], BF16))
            dencb = ea(nc.sbuf_tensor("A_denc", [1, 2 * TTILE], F32))
            rb16 = ea(nc.sbuf_tensor("A_rb16", [1, 2 * TTILE], BF16))
            obuf = ea(nc.sbuf_tensor("A_obuf", [128, 4 * TTILE], BF16))
            padt = ea(nc.sbuf_tensor("A_pad", [128, 8], F32))
            pbig0 = ea(nc.psum_tensor("A_p0", [128, 2048], F32))
            pbig1 = ea(nc.psum_tensor("A_p1", [128, 2048], F32))
            tpV = [pbig1[:, k * 512:(k + 1) * 512].bitcast(BF16)
                   for k in range(4)]

            em = Emitter()

            em.op("sync", lambda E: E.dma_start(out=ident[:, :],
                                                in_=ident_h[:, :]),
                  inc=xl, amt=16)
            em.op("sync", lambda E: E.dma_start(out=onec[:, :],
                                                in_=onec_h[:, :]),
                  inc=xl, amt=16)
            em.op("sync", lambda E: E.dma_start(out=oner[:, :],
                                                in_=oner_h[:, :]),
                  inc=xl, amt=16)
            c_dp = em.op("sync", lambda E: E.dma_start(out=dpt[:, :],
                                                       in_=dp_h[:, :]),
                         inc=xl, amt=16)
            em.wait("scalar", xl, c_dp)
            s_e = em.op("scalar", lambda E: E.activation(
                out=sdec[:, :], in_=dpt[:, :], func=AF.Exp, scale=-1.0), inc=se)
            em.wait("vector", se, s_e)
            c = em.op("vector", lambda E: E.tensor_scalar(
                sdec[:, :], sdec[:, :], 1.0, None, OP.add), inc=ve)
            em.wait("vector", ve, c)
            c = em.op("vector", lambda E: E.reciprocal(sdec[:, :], sdec[:, :]),
                      inc=ve)
            em.wait("vector", ve, c)
            em.op("vector", lambda E: E.memset(dect[:, :], 0.0))
            em.op("vector", lambda E: E.memset(kvc[:, :], 0.0))
            em.op("vector", lambda E: E.memset(kcc[:, :], 0.0))
            for h in range(HLOC):
                em.op("vector", lambda E, h=h: E.tensor_scalar(
                    dect[:, h * TTILE:(h + 1) * TTILE],
                    dect[:, h * TTILE:(h + 1) * TTILE],
                    sdec[:, h:h + 1], None, OP.add))

            def wh_dma(E, ghi):
                h = ghi % HLOC
                slot = ghi % 2
                return E.dma_start(
                    out=whead[:, slot * NKC * 384:(slot + 1) * NKC * 384],
                    in_=wqkv_h[h, :, :])
            c_wh = {0: em.op("gpsimd", lambda E: wh_dma(E, 0), inc=wd, amt=16),
                    1: em.op("gpsimd", lambda E: wh_dma(E, 1), inc=wd, amt=16)}

            def x_dma(E, gc):
                slot = gc % 2
                r = gc * 128
                return E.dma_start(out=xbuf[:, slot * D:(slot + 1) * D],
                                   in_=xa_h[r:r + 128, :])
            c_xl = {0: em.op("sync", lambda E: x_dma(E, 0), inc=xl, amt=16),
                    1: em.op("sync", lambda E: x_dma(E, 1), inc=xl, amt=16)}

            cp_done_g = {}
            cp_done_tile = {}
            tp_pe = {}
            qcnt = {}
            dencnt = {}
            rbcnt = {}
            kv_abs = {}
            prod_abs = {}
            denc_abs = {}
            rb16_abs = {}
            omul_abs = {}
            odma_abs = {}
            eq_abs = {}

            def plan_prep_tile(st_a, tl):
                gt = st_a * 4 + tl
                for c in range(4):
                    gc = gt * 4 + c
                    slot = gc % 2
                    X = xbuf[:, slot * D:(slot + 1) * D]
                    OUTt = lntm[:, slot * D:(slot + 1) * D]
                    so = (gc % 2) * 6
                    pre = [("vector", xl, c_xl[gc]), ("scalar", xl, c_xl[gc])]
                    if (gc - 2, NKC - 1) in tp_pe:
                        pre.append(("scalar", pe, tp_pe[(gc - 2, NKC - 1)]))
                    v2, s1 = ln_plan(em, X, OUTt, OUTt, stat, so, pre)
                    if gc + 2 < NTA * 4:
                        em.wait("sync", ve, v2)
                        em.wait("sync", se, s1)
                        c_xl[gc + 2] = em.op(
                            "sync", lambda E, g=gc + 2: x_dma(E, g),
                            inc=xl, amt=16)
                    em.wait("tensor", ve, v2)
                    if gc == st_a * 16 and st_a > 0:
                        em.wait("tensor", ve, kv_abs[(st_a - 1, NSTEP - 1)])
                        em.wait("tensor", ve, omul_abs[(st_a - 1, NSTEP - 1)])
                    toff = tl * 512 + c * 128
                    for kc in range(NKC):
                        prevk = (gc, kc - 4) if kc >= 4 else (gc - 1, NKC - 4 + kc)
                        if prevk in cp_done_g:
                            em.wait("tensor", ve, cp_done_g[prevk])
                        tpv = tpV[kc % 4]
                        f = (lambda E, slot=slot, kc=kc, tpv=tpv: E.transpose(
                            tpv[:, 0:128],
                            lntm[:, slot * D + kc * 128:
                                 slot * D + (kc + 1) * 128],
                            ident[:, :]))
                        tp_pe[(gc, kc)] = em.op("tensor", f, inc=pe)
                        em.wait("vector", pe, tp_pe[(gc, kc)])
                        dst = lnfm[:, kc * TS + toff:kc * TS + toff + 128]
                        cp_done_g[(gc, kc)] = em.op(
                            "vector", lambda E, tpv=tpv, dst=dst:
                            E.tensor_copy(dst, tpv[:, 0:128]), inc=ve)
                    cp_done_tile[(st_a, tl)] = cp_done_g[(gc, NKC - 1)]

            def plan_heads(st_a):
                for it in range(NSTEP + 3):
                    s, sm1, sm2, sm3 = it, it - 1, it - 2, it - 3

                    if s < NSTEP:
                        h, tl = divmod(s, 4)
                        ghi = st_a * HLOC + h
                        pq = pbig0 if s % 2 == 0 else pbig1
                        em.wait("tensor", ve, cp_done_tile[(st_a, tl)])
                        if s == 1:
                            em.wait("tensor", ve, cp_done_tile[(st_a, 3)])
                        em.wait("tensor", wd, c_wh[ghi])
                        prev_user = (st_a, s - 2) if s - 2 >= 0 else \
                            ((st_a - 1, NSTEP - 2 + s) if st_a > 0 else None)
                        if prev_user in kv_abs:
                            em.wait("tensor", ve, kv_abs[prev_user])
                        wslot = ghi % 2
                        cnts = []
                        for fc3 in range(3):
                            for kc in range(NKC):
                                f = (lambda E, pq=pq, fc3=fc3, kc=kc,
                                     wslot=wslot, tl=tl: E.matmul(
                                         pq[:, fc3 * 512:(fc3 + 1) * 512],
                                         whead[:, wslot * NKC * 384 +
                                               (kc * 3 + fc3) * 128:
                                               wslot * NKC * 384 +
                                               (kc * 3 + fc3) * 128 + 128],
                                         lnfm[:, kc * TS + tl * 512:
                                              kc * TS + (tl + 1) * 512],
                                         start=(kc == 0), stop=(kc == NKC - 1)))
                                if kc == NKC - 1:
                                    cnts.append(em.op("tensor", f, inc=pe))
                                else:
                                    em.op("tensor", f)
                        qcnt[(st_a, s)] = cnts
                    if 0 <= sm1 < NSTEP:
                        pq = pbig0 if sm1 % 2 == 0 else pbig1
                        esl = sm1 % 2
                        prev_phi = (st_a, sm1 - 2) if sm1 - 2 >= 0 else \
                            ((st_a - 1, NSTEP - 2 + sm1) if st_a > 0 else None)
                        if prev_phi in kv_abs:
                            em.wait("scalar", ve, kv_abs[prev_phi])
                        em.wait("scalar", pe, qcnt[(st_a, sm1)][1])
                        em.op("scalar", lambda E, pq=pq, esl=esl: E.activation(
                            out=eqb[:, esl * 512:(esl + 1) * 512],
                            in_=pq[:, 0:512], func=AF.Relu, scale=-1.0))
                        em.op("scalar", lambda E, pq=pq, esl=esl: E.activation(
                            out=ekb[:, esl * 512:(esl + 1) * 512],
                            in_=pq[:, 512:1024], func=AF.Relu, scale=-1.0))
                        em.op("scalar", lambda E, esl=esl: E.activation(
                            out=eqb[:, esl * 512:(esl + 1) * 512],
                            in_=eqb[:, esl * 512:(esl + 1) * 512],
                            func=AF.Exp, scale=-1.0))
                        eq_abs[(st_a, sm1)] = em.op(
                            "scalar", lambda E, esl=esl: E.activation(
                                out=ekb[:, esl * 512:(esl + 1) * 512],
                                in_=ekb[:, esl * 512:(esl + 1) * 512],
                                func=AF.Exp, scale=-1.0), inc=se)

                    d1 = 0 <= sm1 < NSTEP
                    d2 = 0 <= sm2 < NSTEP
                    d3 = 0 <= sm3 < NSTEP

                    def vpad():
                        em.op("vector", lambda E: E.memset(padt[:, :], 0.0))

                    if d1:
                        h1, _ = divmod(sm1, 4)
                        pq1 = pbig0 if sm1 % 2 == 0 else pbig1
                        esl1 = sm1 % 2
                        q41 = sm1 % 4
                        em.wait("vector", se, eq_abs[(st_a, sm1)])
                        em.wait("vector", pe, qcnt[(st_a, sm1)][2])
                        em.op("vector", lambda E, pq=pq1, esl=esl1, q4=q41:
                              E.scalar_tensor_tensor(
                                  phiq[:, q4 * 512:(q4 + 1) * 512],
                                  pq[:, 0:512], 0.0,
                                  eqb[:, esl * 512:(esl + 1) * 512],
                                  OP.max, OP.add))
                        em.op("vector", lambda E, pq=pq1, esl=esl1:
                              E.scalar_tensor_tensor(
                                  phik[:, esl * 512:(esl + 1) * 512],
                                  pq[:, 512:1024], 0.0,
                                  ekb[:, esl * 512:(esl + 1) * 512],
                                  OP.max, OP.add))
                    if d2:
                        esl2 = sm2 % 2
                        em.wait("vector", pe, dencnt[(st_a, sm2)])
                        prev_rbmm = (st_a, sm2 - 2) if sm2 - 2 >= 0 else \
                            ((st_a - 1, NSTEP - 2 + sm2) if st_a > 0 else None)
                        if prev_rbmm in rbcnt:
                            em.wait("vector", pe, rbcnt[prev_rbmm])
                        em.op("vector", lambda E, esl=esl2: E.tensor_scalar(
                            dencb[0:1, esl * 512:(esl + 1) * 512],
                            pbig0[0:1, 1536:2048], EPS_DEN, None, OP.max))
                    elif d1:
                        vpad()
                    if d1:
                        kv_abs[(st_a, sm1)] = em.op(
                            "vector", lambda E, pq=pq1, esl=esl1:
                            E.tensor_tensor(
                                kvb[:, esl * 512:(esl + 1) * 512],
                                phik[:, esl * 512:(esl + 1) * 512],
                                pq[:, 1024:1536], OP.mult), inc=ve)
                    elif d2:
                        vpad()
                    if d2:
                        denc_abs[(st_a, sm2)] = em.op(
                            "vector", lambda E, esl=esl2: E.reciprocal(
                                dencb[0:1, esl * 512:(esl + 1) * 512],
                                dencb[0:1, esl * 512:(esl + 1) * 512]), inc=ve)
                    elif d1:
                        vpad()
                    if d1:
                        kvs_cnt = em.op("vector", lambda E, esl=esl1, q4=q41,
                                        h=h1: E.tensor_tensor_scan(
                                            kvs[:, q4 * 512:(q4 + 1) * 512],
                                            dect[:, h * TTILE:(h + 1) * TTILE],
                                            kvb[:, esl * 512:(esl + 1) * 512],
                                            kvc[:, h:h + 1], OP.mult, OP.add),
                                        inc=ve)
                    elif d2:
                        vpad()
                    if d2:
                        rb16_abs[(st_a, sm2)] = em.op(
                            "vector", lambda E, esl=esl2: E.tensor_copy(
                                rb16[0:1, esl * 512:(esl + 1) * 512],
                                dencb[0:1, esl * 512:(esl + 1) * 512]), inc=ve)
                    elif d1:
                        vpad()
                    if d1:
                        em.wait("vector", ve, kvs_cnt)
                        em.op("vector", lambda E, q4=q41, h=h1: E.tensor_copy(
                            kvc[:, h:h + 1],
                            kvs[:, q4 * 512 + 511:q4 * 512 + 512]))
                    if d3:
                        h3_, tl3 = divmod(sm3, 4)
                        q43 = sm3 % 4
                        osl = sm3 % 4
                        em.wait("vector", pe, rbcnt[(st_a, sm3)])
                        prev_o = (st_a, sm3 - 4) if sm3 - 4 >= 0 else \
                            ((st_a - 1, NSTEP - 4 + sm3) if st_a > 0 else None)
                        if prev_o in odma_abs:
                            em.wait("vector", st, odma_abs[prev_o])
                        em.op("vector", lambda E, q4=q43: E.tensor_tensor(
                            t0b[:, :], phiq[:, q4 * 512:(q4 + 1) * 512],
                            kvs[:, q4 * 512:(q4 + 1) * 512], OP.mult))
                    if d1:
                        kss_cnt = em.op("vector", lambda E, esl=esl1, h=h1:
                                        E.tensor_tensor_scan(
                                            kss[:, esl * 512:(esl + 1) * 512],
                                            dect[:, h * TTILE:(h + 1) * TTILE],
                                            phik[:, esl * 512:(esl + 1) * 512],
                                            kcc[:, h:h + 1], OP.mult, OP.add),
                                        inc=ve)
                    elif d3:
                        vpad()
                    if not d3 and d1:
                        vpad()
                    if d3:
                        omul_abs[(st_a, sm3)] = em.op(
                            "vector", lambda E, osl=osl: E.tensor_tensor(
                                obuf[:, osl * 512:(osl + 1) * 512],
                                t0b[:, :], pbig1[:, 1536:2048], OP.mult),
                            inc=ve)
                        em.wait("sync", ve, omul_abs[(st_a, sm3)])
                        gcol = st_a * TS + tl3 * 512
                        odma_abs[(st_a, sm3)] = em.op(
                            "sync", lambda E, h=h3_, gcol=gcol, osl=osl:
                            E.dma_start(
                                out=o_mine[h * 128:(h + 1) * 128,
                                           gcol:gcol + 512],
                                in_=obuf[:, osl * 512:(osl + 1) * 512]),
                            inc=st, amt=16)
                    if d1:
                        em.wait("vector", ve, kss_cnt)
                        em.op("vector", lambda E, esl=esl1, h=h1: E.tensor_copy(
                            kcc[:, h:h + 1],
                            kss[:, esl * 512 + 511:esl * 512 + 512]))
                        prev_prod = (st_a, sm1 - 2) if sm1 - 2 >= 0 else \
                            ((st_a - 1, NSTEP - 2 + sm1) if st_a > 0 else None)
                        if prev_prod in dencnt:
                            em.wait("vector", pe, dencnt[prev_prod])
                        prod_abs[(st_a, sm1)] = em.op(
                            "vector", lambda E, esl=esl1, q4=q41:
                            E.tensor_tensor(
                                prodb[:, esl * 512:(esl + 1) * 512],
                                phiq[:, q4 * 512:(q4 + 1) * 512],
                                kss[:, esl * 512:(esl + 1) * 512],
                                OP.mult), inc=ve)
                    if d1:
                        em.wait("tensor", ve, prod_abs[(st_a, sm1)])
                        prev_den = (st_a, sm1 - 1) if sm1 - 1 >= 0 else \
                            ((st_a - 1, NSTEP - 1) if st_a > 0 else None)
                        if prev_den in denc_abs:
                            em.wait("tensor", ve, denc_abs[prev_den])
                        dencnt[(st_a, sm1)] = em.op(
                            "tensor", lambda E, sl=sm1 % 2: E.matmul(
                                pbig0[0:1, 1536:2048], onec[:, :],
                                prodb[:, sl * 512:(sl + 1) * 512],
                                start=True, stop=True), inc=pe)
                    if 0 <= sm2 < NSTEP:
                        em.wait("tensor", ve, rb16_abs[(st_a, sm2)])
                        if sm2 == 0:
                            em.wait("tensor", ve, cp_done_tile[(st_a, 3)])
                        prev_rb = (st_a, sm2 - 1) if sm2 - 1 >= 0 else \
                            ((st_a - 1, NSTEP - 1) if st_a > 0 else None)
                        if prev_rb in omul_abs:
                            em.wait("tensor", ve, omul_abs[prev_rb])
                        rbcnt[(st_a, sm2)] = em.op(
                            "tensor", lambda E, sl=sm2 % 2: E.matmul(
                                pbig1[:, 1536:2048], oner[:, :],
                                rb16[0:1, sl * 512:(sl + 1) * 512],
                                start=True, stop=True), inc=pe)

                    if s < NSTEP and s % 4 == 3:
                        ghi = st_a * HLOC + s // 4
                        if ghi + 2 < NSTA * HLOC:
                            em.wait("gpsimd", pe, qcnt[(st_a, s)][2])
                            c_wh[ghi + 2] = em.op(
                                "gpsimd", lambda E, g=ghi + 2: wh_dma(E, g),
                                inc=wd, amt=16)

            for st_a in range(NSTA):
                if st_a > 0:
                    em.wait("vector", pe, qcnt[(st_a - 1, NSTEP - 1)][2])
                for tl in range(4):
                    plan_prep_tile(st_a, tl)
                plan_heads(st_a)

            st_total_A = st.n
            if debug:
                em.wait("gpsimd", st, st_total_A)
                em.op("gpsimd", lambda E: E.dma_start(
                    out=dbg["omine"][:, :], in_=o_mine[:, :]), inc=wd, amt=16)
            with nc.Block("blkA") as blkA:
                em.register(blkA)
        if "C" not in phases:
            return nc

        with ExitStack() as SC:
            ec = SC.enter_context
            woutb = ec(nc.sbuf_tensor("C_wout", [128, NKC_C * D], BF16))
            otb = ec(nc.sbuf_tensor("C_o", [128, 2 * NKC_C * 512], BF16))
            pstg = ec(nc.sbuf_tensor("C_pstg", [128, 2 * D], BF16))
            cbig0 = ec(nc.psum_tensor("C_p0", [128, 2048], F32))
            cbig1 = ec(nc.psum_tensor("C_p1", [128, 2048], F32))

            em = Emitter()

            em.wait("gpsimd", st, st_total_A)
            c_wout = em.op("gpsimd", lambda E: E.dma_start(
                out=woutb[:, :], in_=wout_h[:, :]), inc=wd, amt=16)

            em.wait("sync", st, st_total_A)

            TILE_ORDER = list(range(16))

            def ot_dma(E, tile, slot):
                goff = tile * 512
                src = o_mine[:, :].rearrange("(kc p) t -> p kc t", p=128)[
                    :, :, goff:goff + 512]
                dstv = otb[:, slot * NKC_C * 512:(slot + 1) * NKC_C * 512]
                dst = dstv.rearrange("p (kc t) -> p kc t", kc=NKC_C)
                return E.dma_start(out=dst, in_=src)

            c_ot = {0: em.op("sync", lambda E: ot_dma(E, TILE_ORDER[0], 0),
                             inc=xl, amt=16),
                    1: em.op("sync", lambda E: ot_dma(E, TILE_ORDER[1], 1),
                             inc=xl, amt=16)}
            mm_cnt = {}
            evac_cnt = {}
            pdma_cnt = {}
            c_rs = {}
            em.wait("tensor", wd, c_wout)
            pidx = 0
            for pos, tile in enumerate(TILE_ORDER):
                slot = pos % 2
                em.wait("tensor", xl, c_ot[pos])
                for tc in range(4):
                    tcg = tile * 4 + tc
                    rowc = tcg
                    cp = cbig0 if pidx % 2 == 0 else cbig1
                    if pidx - 2 >= 0:
                        em.wait("tensor", ve, evac_cnt[pidx - 2])
                    for dt in range(4):
                        for kc in range(NKC_C):
                            f = (lambda E, cp=cp, dt=dt, kc=kc, slot=slot,
                                 tc=tc: E.matmul(
                                     cp[:, dt * 512:(dt + 1) * 512],
                                     otb[:, slot * NKC_C * 512 + kc * 512 +
                                         tc * 128:slot * NKC_C * 512 +
                                         kc * 512 + tc * 128 + 128],
                                     woutb[:, kc * D + dt * 512:
                                           kc * D + (dt + 1) * 512],
                                     start=(kc == 0), stop=(kc == NKC_C - 1)))
                            if dt == 3 and kc == NKC_C - 1:
                                mm_cnt[pidx] = em.op("tensor", f, inc=pe)
                            else:
                                em.op("tensor", f)
                    em.wait("vector", pe, mm_cnt[pidx])
                    if pidx - 2 >= 0:
                        em.wait("vector", st, pdma_cnt[pidx - 2])
                    evac_cnt[pidx] = em.op(
                        "vector", lambda E, cp=cp, sl=pidx % 2: E.tensor_copy(
                            pstg[:, sl * D:(sl + 1) * D], cp[:, :]), inc=ve)
                    em.wait("sync", ve, evac_cnt[pidx])
                    pdma_cnt[pidx] = em.op(
                        "sync", lambda E, rowc=rowc, sl=pidx % 2: E.dma_start(
                            out=partial_h[rowc * 128:(rowc + 1) * 128, :],
                            in_=pstg[:, sl * D:(sl + 1) * D]), inc=st, amt=16)
                    pidx += 1
                if pos + 2 < 16:
                    em.wait("sync", pe, mm_cnt[pos * 4 + 3])
                    c_ot[pos + 2] = em.op(
                        "sync", lambda E, t=TILE_ORDER[pos + 2],
                        sl=(pos + 2) % 2: ot_dma(E, t, sl), inc=xl, amt=16)
            em.wait("gpsimd", st, pdma_cnt[63])
            c_rs[0] = em.op(
                "gpsimd", lambda E: E.collective_compute(
                    "ReduceScatter", OP.add,
                    replica_groups=[[0, 1], [2, 3], [4, 5], [6, 7]],
                    ins=[partial_h[:, :]], outs=[attnf_h[:, :]]), inc=cc)
            if debug:
                em.wait("gpsimd", cc, c_rs[0])
                em.op("gpsimd", lambda E: E.dma_start(
                    out=dbg["attnf"][:, :], in_=attnf_h[:, :]), inc=wd, amt=16)
            c1_pe_last = mm_cnt[63]
            c1_ve_last = evac_cnt[63]
            c1_st_last = pdma_cnt[63]
            with nc.Block("blkC") as blkC:
                em.register(blkC)
        if "D" not in phases:
            return nc

        with ExitStack() as SD:
            ed = SD.enter_context
            gateb = ed(nc.sbuf_tensor("D_gw", [128, NKC * 16], BF16))
            ln2fm = ed(nc.sbuf_tensor("D_ln2", [128, NKC * TS], BF16))
            hb = ed(nc.sbuf_tensor("D_h", [128, NFC * TS], BF16))
            wffb = ed(nc.sbuf_tensor("D_wff", [128, 2 * NKC * 2 * 128], BF16))
            w2b = ed(nc.sbuf_tensor("D_w2", [128, 4 * NFC * 512], BF16))
            ytmp = ed(nc.sbuf_tensor("D_y", [128, 2 * D], F32))
            lnt_b = ytmp[:, 0:D].bitcast(BF16)[:, 0:D]
            xcv = ytmp[:, D:2 * D]
            lnt = ed(nc.sbuf_tensor("D_lnt", [128, D], F32))
            xmb2 = ed(nc.sbuf_tensor("D_xm", [128, D], F32))
            silu_sb = ed(nc.sbuf_tensor("D_silu", [128, 512], F32))
            gates = ed(nc.sbuf_tensor("D_g", [128, 16 * NE], F32))
            gsc = ed(nc.sbuf_tensor("D_gs", [128, 16], F32))
            gsc2 = ed(nc.sbuf_tensor("D_gs2", [128, 16], F32))
            statd = ed(nc.sbuf_tensor("D_stat", [128, 24], F32))
            afv = hb[:, 0:D]
            dbig0 = ed(nc.psum_tensor("D_p0", [128, 2048], F32))
            dbig1 = ed(nc.psum_tensor("D_p1", [128, 2048], F32))
            tpDV = [dbig1[:, k * 512:(k + 1) * 512].bitcast(BF16)
                    for k in range(4)]

            em = Emitter()

            em.wait("gpsimd", st, c1_st_last)
            em.wait("gpsimd", pe, c1_pe_last)
            c_gw = em.op("gpsimd", lambda E: E.dma_start(
                out=gateb[:, :], in_=gatew_h[:, :]), inc=wd, amt=16)

            NW = NST_D * NE * NFC

            def wff_dma(E, g):
                ste, fc = divmod(g, NFC)
                e = ste % NE
                slot = g % 2
                return E.dma_start(
                    out=wffb[:, slot * NKC * 256:(slot + 1) * NKC * 256],
                    in_=wff_hs[e][fc, :, :])

            def w2_dma(E, ge):
                e = ge % NE
                return E.dma_start(out=w2b[:, :], in_=w2t_hs[e][:, :])

            c_wffc = {0: em.op("gpsimd", lambda E: wff_dma(E, 0), inc=wd, amt=16),
                      1: em.op("gpsimd", lambda E: wff_dma(E, 1), inc=wd, amt=16)}
            c_w2c = {0: em.op("gpsimd", lambda E: w2_dma(E, 0), inc=wd, amt=16)}


            hcnt = {}
            silucnt = {}
            hevcnt = {}
            ymcnt = {}
            yevcnt = {}
            ydma = {}
            ylast_pe = {}
            tpd = {}
            cpd = {}
            gate_ve = {}
            gate_se = {}
            gate_nmx = {}
            c_af2 = {}
            c_xc2 = {}
            c_xmst = {}
            v_add = {}
            prev_b = {}
            c_yl = {}
            c_xm3 = {}
            outdma = {}
            addc = {}

            prev_norm = {}
            prev_a = {}

            for st_d in range(NST_D):
                for ch in range(16):
                    gch = st_d * 16 + ch
                    q = gch // 8
                    if ch == 0:
                        if st_d > 0:
                            em.wait("sync", pe, ylast_pe[st_d * NE - 1])
                            em.wait("sync", st, outdma[(st_d - 1, 15)])
                        em.wait("sync", cc, 1)
                    if ch > 0:
                        va, sa_ = prev_a[gch - 1]
                        em.wait("sync", ve, va)
                        em.wait("sync", se, sa_)
                        em.wait("sync", ve, v_add[gch - 1])
                    elif ch == 0 and q > 0:
                        pass
                    c_af2[gch] = em.op(
                        "sync", lambda E, g=gch: E.dma_start(
                            out=afv[:, :],
                            in_=attnf_h[g * 128:(g + 1) * 128, :]),
                        inc=xl, amt=16)
                    c_xc2[gch] = em.op(
                        "sync", lambda E, g=gch: E.dma_start(
                            out=xcv[:, :],
                            in_=xc_h[g * 128:(g + 1) * 128, :]),
                        inc=xl, amt=16)
                    pre = [("vector", xl, c_af2[gch]),
                           ("scalar", xl, c_af2[gch])]
                    if gch - 1 in v_add:
                        pre.append(("scalar", ve, v_add[gch - 1]))
                    elif gch == 0:
                        pre.append(("scalar", ve, c1_ve_last))
                        pre.append(("vector", ve, c1_ve_last))
                    if ch == 0 and st_d > 0:
                        pre.append(("scalar", st, outdma[(st_d - 1, 15)]))
                        pre.append(("vector", st, outdma[(st_d - 1, 15)]))
                    v2a, s1a = ln_plan(em, afv[:, :], lnt[:, :], lnt[:, :],
                                       statd, (gch % 2) * 6, pre)
                    prev_a[gch] = (v2a, s1a)
                    em.wait("vector", xl, c_xc2[gch])
                    em.wait("vector", ve, v2a)
                    if gch - 1 in prev_norm:
                        vb, sb_ = prev_norm[gch - 1]
                        em.wait("vector", ve, vb)
                        em.wait("vector", se, sb_)
                    v_add[gch] = em.op(
                        "vector", lambda E: E.tensor_tensor(
                            xmb2[:, :], lnt[:, :], xcv[:, :], OP.add), inc=ve)
                    em.wait("sync", ve, v_add[gch])
                    c_xmst[gch] = em.op(
                        "sync", lambda E, g=gch: E.dma_start(
                            out=xmid_h[g * 128:(g + 1) * 128, :],
                            in_=xmb2[:, :]), inc=st, amt=16)
                    pre = [("vector", ve, v_add[gch]),
                           ("scalar", ve, v_add[gch])]
                    if (gch - 1, NKC - 1) in tpd:
                        pre.append(("scalar", pe, tpd[(gch - 1, NKC - 1)]))
                    v2, s1 = ln_plan(em, xmb2[:, :], lnt_b[:, :], lnt_b[:, :],
                                     statd, 12 + (gch % 2) * 6, pre)
                    prev_norm[gch] = (v2, s1)
                    em.wait("tensor", ve, v2)
                    if ch == 0:
                        if st_d == 0:
                            em.wait("tensor", ve, c1_ve_last)
                        else:
                            em.wait("tensor", ve,
                                    hevcnt[(st_d * NE - 1, NFC - 1, 3)])
                        if st_d > 0:
                            em.wait("vector", pe,
                                    hcnt[(st_d * NE - 1, NFC - 1, 3)])
                    for kc in range(NKC):
                        prevk = (gch, kc - 4) if kc >= 4 else \
                            (gch - 1, NKC - 4 + kc)
                        if prevk in cpd:
                            em.wait("tensor", ve, cpd[prevk])
                        tpv = tpDV[kc % 4]
                        f = (lambda E, kc=kc, tpv=tpv: E.transpose(
                            tpv[:, 0:128],
                            lnt_b[:, kc * 128:(kc + 1) * 128], ident[:, :]))
                        tpd[(gch, kc)] = em.op("tensor", f, inc=pe)
                        em.wait("vector", pe, tpd[(gch, kc)])
                        dst = ln2fm[:, kc * TS + ch * 128:
                                    kc * TS + ch * 128 + 128]
                        cpd[(gch, kc)] = em.op(
                            "vector", lambda E, tpv=tpv, dst=dst:
                            E.tensor_copy(dst, tpv[:, 0:128]), inc=ve)
                    em.wait("tensor", ve, cpd[(gch, NKC - 1)])
                    em.wait("tensor", wd, c_gw)
                    if ch == 0:
                        if st_d == 0:
                            em.wait("tensor", ve, c1_ve_last)
                        else:
                            em.wait("tensor", ve,
                                    yevcnt[(st_d * NE - 1, 62)])
                            em.wait("tensor", ve,
                                    yevcnt[(st_d * NE - 1, 63)])
                            em.wait("tensor", se,
                                    silucnt[(st_d * NE - 1, NFC - 1, 2)])
                            em.wait("tensor", se,
                                    silucnt[(st_d * NE - 1, NFC - 1, 3)])
                    else:
                        em.wait("tensor", ve, gate_nmx[gch - 1])
                        em.wait("tensor", se, gate_se[gch - 1])
                    for kc in range(NKC):
                        f = (lambda E, kc=kc, ch=ch: E.matmul(
                            dbig0[:, 0:16],
                            ln2fm[:, kc * TS + ch * 128:kc * TS + ch * 128 + 128],
                            gateb[:, kc * 16:(kc + 1) * 16],
                            start=(kc == 0), stop=(kc == NKC - 1)))
                        if kc == NKC - 1:
                            g_pe = em.op("tensor", f, inc=pe)
                        else:
                            em.op("tensor", f)
                    em.wait("vector", pe, g_pe)
                    gate_nmx[gch] = em.op("vector", lambda E: E.tensor_reduce(
                        statd[:, 10:11], dbig0[:, 0:16], AX.X, OP.max,
                        negate=True), inc=ve)
                    em.wait("scalar", ve, gate_nmx[gch])
                    em.wait("scalar", pe, g_pe)
                    gate_se[gch] = em.op("scalar", lambda E: E.activation(
                        out=gsc[:, :], in_=dbig0[:, 0:16], func=AF.Exp,
                        bias=statd[:, 10:11]), inc=se)
                    em.wait("vector", se, gate_se[gch])
                    em.op("vector", lambda E: E.tensor_reduce(
                        statd[:, 11:12], gsc[:, :], AX.X, OP.add))
                    c = em.op("vector", lambda E: E.tensor_reduce(
                        statd[:, 10:11], gsc[:, :], AX.X, OP.max), inc=ve)
                    em.wait("vector", ve, c)
                    em.op("vector", lambda E: E.tensor_scalar(
                        gsc2[:, :], gsc[:, :], statd[:, 10:11], None,
                        OP.is_equal))
                    em.op("vector", lambda E: E.reciprocal(
                        statd[:, 11:12], statd[:, 11:12]))
                    c = em.op("vector", lambda E: E.scalar_tensor_tensor(
                        gsc2[:, :], gsc2[:, :], -BIG, gsc[:, :],
                        OP.mult, OP.add), inc=ve)
                    em.wait("vector", ve, c)
                    c = em.op("vector", lambda E: E.tensor_reduce(
                        statd[:, 10:11], gsc2[:, :], AX.X, OP.max), inc=ve)
                    em.op("vector", lambda E, ch=ch: E.memset(
                        gates[:, ch * NE + 16:ch * NE + 18], 1.0))
                    em.wait("vector", ve, c)
                    c = em.op("vector", lambda E: E.tensor_scalar(
                        gsc2[:, :], gsc[:, :], statd[:, 10:11], None,
                        OP.is_ge), inc=ve)
                    c2 = em.op("vector", lambda E: E.tensor_scalar(
                        gsc[:, :], gsc[:, :], statd[:, 11:12], None, OP.mult),
                        inc=ve)
                    em.wait("vector", ve, c)
                    em.wait("vector", ve, c2)
                    gate_ve[gch] = em.op(
                        "vector", lambda E, ch=ch: E.tensor_tensor(
                            gates[:, ch * NE:ch * NE + 16], gsc[:, :],
                            gsc2[:, :], OP.mult), inc=ve)

                for e in range(NE):
                    ste = st_d * NE + e
                    for fc in range(NFC):
                        g = ste * NFC + fc
                        em.wait("tensor", wd, c_wffc[g])
                        wsl = g % 2
                        for tt in range(4):
                            if fc == 0:
                                if e == 0:
                                    if tt == 0:
                                        em.wait("tensor", ve,
                                                gate_ve[st_d * 16 + 15])
                                        em.wait("tensor", se,
                                                gate_se[st_d * 16 + 15])
                                    if st_d > 0:
                                        psl = st_d * NE - 1
                                        if tt <= 1:
                                            em.wait("tensor", ve,
                                                    yevcnt[(psl, 62 + tt)])
                                        em.wait("tensor", se,
                                                silucnt[(psl, NFC - 1, tt)])
                                        em.wait("tensor", ve,
                                                hevcnt[(psl, NFC - 1, tt)])
                                    em.wait("tensor", ve,
                                            cpd[(st_d * 16 + 15, NKC - 1)])
                                else:
                                    if tt <= 1:
                                        em.wait("tensor", ve,
                                                yevcnt[(ste - 1, 62 + tt)])
                                    em.wait("tensor", se,
                                            silucnt[(ste - 1, NFC - 1, tt)])
                                    em.wait("tensor", ve,
                                            hevcnt[(ste - 1, NFC - 1, tt)])
                            else:
                                em.wait("tensor", se,
                                        silucnt[(ste, fc - 1, tt)])
                                em.wait("tensor", ve,
                                        hevcnt[(ste, fc - 1, tt)])
                            for kc in range(NKC):
                                f1 = (lambda E, tt=tt, kc=kc, wsl=wsl:
                                      E.matmul(
                                          dbig0[:, tt * 512:(tt + 1) * 512],
                                          wffb[:, wsl * NKC * 256 +
                                               kc * 256:wsl * NKC * 256 +
                                               kc * 256 + 128],
                                          ln2fm[:, kc * TS + tt * 512:
                                                kc * TS + (tt + 1) * 512],
                                          start=(kc == 0),
                                          stop=(kc == NKC - 1)))
                                em.op("tensor", f1)
                                f3 = (lambda E, tt=tt, kc=kc, wsl=wsl:
                                      E.matmul(
                                          dbig1[:, tt * 512:(tt + 1) * 512],
                                          wffb[:, wsl * NKC * 256 +
                                               kc * 256 + 128:
                                               wsl * NKC * 256 + kc * 256 + 256],
                                          ln2fm[:, kc * TS + tt * 512:
                                                kc * TS + (tt + 1) * 512],
                                          start=(kc == 0),
                                          stop=(kc == NKC - 1)))
                                if kc == NKC - 1:
                                    hcnt[(ste, fc, tt)] = em.op(
                                        "tensor", f3, inc=pe)
                                else:
                                    em.op("tensor", f3)
                            sid = (fc * 4 + tt)
                            ssl = 0
                            em.wait("scalar", pe, hcnt[(ste, fc, tt)])
                            psid = sid - 1
                            if psid >= 0:
                                em.wait("scalar", ve,
                                        hevcnt[(ste, psid // 4, psid % 4)])
                            elif ste > 0:
                                em.wait("scalar", ve,
                                        hevcnt[(ste - 1, NFC - 1, 3)])
                            silucnt[(ste, fc, tt)] = em.op(
                                "scalar", lambda E, tt=tt, ssl=ssl:
                                E.activation(
                                    out=silu_sb[:, ssl * 512:(ssl + 1) * 512],
                                    in_=dbig0[:, tt * 512:(tt + 1) * 512],
                                    func=AF.Silu), inc=se)
                            em.wait("vector", se, silucnt[(ste, fc, tt)])
                            if fc == 0 and tt == 0 and ste > 0:
                                em.wait("vector", pe, ylast_pe[ste - 1])
                            hevcnt[(ste, fc, tt)] = em.op(
                                "vector", lambda E, fc=fc, tt=tt, ssl=ssl:
                                E.tensor_tensor(
                                    hb[:, fc * TS + tt * 512:
                                       fc * TS + (tt + 1) * 512],
                                    silu_sb[:, ssl * 512:(ssl + 1) * 512],
                                    dbig1[:, tt * 512:(tt + 1) * 512],
                                    OP.mult), inc=ve)
                        if g + 2 < NW:
                            em.wait("gpsimd", pe, hcnt[(ste, fc, 3)])
                            c_wffc[g + 2] = em.op(
                                "gpsimd", lambda E, g2=g + 2: wff_dma(E, g2),
                                inc=wd, amt=16)
                    em.wait("tensor", wd, c_w2c[ste])
                    for tc in range(16):
                        for dt in range(4):
                            idx = tc * 4 + dt
                            sl = idx % 2
                            if idx == 0:
                                em.wait("tensor", ve,
                                        hevcnt[(ste, NFC - 1, 3)])
                            if idx <= 1:
                                em.wait("tensor", se,
                                        silucnt[(ste, NFC - 1, idx)])
                            if idx - 2 >= 0:
                                em.wait("tensor", ve, yevcnt[(ste, idx - 2)])
                            for fc in range(NFC):
                                f = (lambda E, sl=sl, fc=fc, tc=tc, dt=dt:
                                     E.matmul(
                                         dbig0[:, sl * 512:(sl + 1) * 512],
                                         hb[:, fc * TS + tc * 128:
                                            fc * TS + tc * 128 + 128],
                                         w2b[:, (dt * NFC + fc) * 512:
                                             (dt * NFC + fc + 1) * 512],
                                         start=(fc == 0),
                                         stop=(fc == NFC - 1)))
                                if fc == NFC - 1:
                                    ymcnt[(ste, idx)] = em.op(
                                        "tensor", f, inc=pe)
                                else:
                                    em.op("tensor", f)
                            em.wait("vector", pe, ymcnt[(ste, idx)])
                            if dt == 0 and (ste, tc - 2) in ydma:
                                em.wait("vector", wd, ydma[(ste, tc - 2)])
                            if dt == 0 and tc <= 1 and ste > 0:
                                if e > 0:
                                    em.wait("vector", wd,
                                            ydma[(ste - 1, 14 + tc)])
                                else:
                                    em.wait("vector", st,
                                            outdma[(st_d - 1, 14 + tc)])
                            yevcnt[(ste, idx)] = em.op(
                                "vector", lambda E, sl=sl, tc=tc, dt=dt, e=e:
                                E.tensor_scalar(
                                    ytmp[:, (tc % 2) * D + dt * 512:
                                         (tc % 2) * D + (dt + 1) * 512],
                                    dbig0[:, sl * 512:(sl + 1) * 512],
                                    gates[:, tc * NE + e:tc * NE + e + 1],
                                    None, OP.mult), inc=ve)
                        em.wait("gpsimd", ve, yevcnt[(ste, tc * 4 + 3)])
                        if e == 0:
                            f = (lambda E, st_d=st_d, tc=tc: E.dma_start(
                                out=y_h[st_d, tc * 128:(tc + 1) * 128, :],
                                in_=ytmp[:, (tc % 2) * D:(tc % 2 + 1) * D]))
                        else:
                            f = (lambda E, st_d=st_d, tc=tc: E.dma_start(
                                out=y_h[st_d, tc * 128:(tc + 1) * 128, :],
                                in_=ytmp[:, (tc % 2) * D:(tc % 2 + 1) * D],
                                accum_op=OP.add))
                        ydma[(ste, tc)] = em.op("gpsimd", f, inc=wd, amt=16)
                    ylast_pe[ste] = ymcnt[(ste, 63)]
                    if ste + 1 < NST_D * NE:
                        em.wait("gpsimd", pe, ylast_pe[ste])
                        c_w2c[ste + 1] = em.op(
                            "gpsimd", lambda E, g2=ste + 1: w2_dma(E, g2),
                            inc=wd, amt=16)

                y_all = ydma[(st_d * NE + NE - 1, 15)]
                em.wait("sync", wd, y_all)

                def yl_dma(E, st_d, ch):
                    sl = ch % 2
                    return E.dma_start(out=ytmp[:, sl * D:(sl + 1) * D],
                                       in_=y_h[st_d, ch * 128:(ch + 1) * 128, :])

                def xm3_dma(E, st_d, ch):
                    gr = st_d * TS + ch * 128
                    return E.dma_start(out=xmb2[:, :],
                                       in_=xmid_h[gr:gr + 128, :])

                c_yl[(st_d, 0)] = em.op(
                    "sync", lambda E, sd=st_d: yl_dma(E, sd, 0),
                    inc=xl, amt=16)
                c_yl[(st_d, 1)] = em.op(
                    "sync", lambda E, sd=st_d: yl_dma(E, sd, 1),
                    inc=xl, amt=16)
                v2p, s1p = prev_norm[st_d * 16 + 15]
                em.wait("sync", ve, v2p)
                em.wait("sync", se, s1p)
                em.wait("sync", st, c_xmst[st_d * 16 + 15])
                c_xm3[(st_d, 0)] = em.op(
                    "sync", lambda E, sd=st_d: xm3_dma(E, sd, 0),
                    inc=xl, amt=16)
                for ch in range(16):
                    sl = ch % 2
                    X = ytmp[:, sl * D:(sl + 1) * D]
                    pre = [("vector", xl, c_yl[(st_d, ch)]),
                           ("scalar", xl, c_yl[(st_d, ch)])]
                    if ch == 0:
                        if st_d > 0:
                            pre.append(("scalar", ve, addc[(st_d - 1, 15)]))
                    else:
                        pre.append(("scalar", ve, addc[(st_d, ch - 1)]))
                    v2, s1 = ln_plan(em, X, lnt[:, :], lnt[:, :], statd,
                                     (ch % 2) * 6, pre)
                    em.wait("vector", xl, c_xm3[(st_d, ch)])
                    em.wait("vector", ve, v2)
                    addc[(st_d, ch)] = em.op(
                        "vector", lambda E, sl=sl: E.tensor_tensor(
                            ytmp[:, sl * D:(sl + 1) * D], lnt[:, :],
                            xmb2[:, :], OP.add), inc=ve)
                    em.wait("sync", ve, addc[(st_d, ch)])
                    gout = st_d * TS + ch * 128
                    outdma[(st_d, ch)] = em.op(
                        "sync", lambda E, gout=gout, sl=sl: E.dma_start(
                            out=out_h[gout:gout + 128, :],
                            in_=ytmp[:, sl * D:(sl + 1) * D]), inc=st, amt=16)
                    if ch + 2 < 16:
                        em.wait("sync", st, outdma[(st_d, ch)])
                        c_yl[(st_d, ch + 2)] = em.op(
                            "sync", lambda E, sd=st_d, c2=ch + 2:
                            yl_dma(E, sd, c2), inc=xl, amt=16)
                    if ch + 1 < 16:
                        em.wait("sync", ve, addc[(st_d, ch)])
                        c_xm3[(st_d, ch + 1)] = em.op(
                            "sync", lambda E, sd=st_d, c2=ch + 1:
                            xm3_dma(E, sd, c2), inc=xl, amt=16)

            if debug:
                em.wait("gpsimd", st, outdma[(NST_D - 1, 15)])
                em.op("gpsimd", lambda E: E.dma_start(
                    out=dbg["y"][:, :, :], in_=y_h[:, :, :]), inc=wd, amt=16)
            em.wait("sync", st, st.n)
            em.wait("gpsimd", wd, wd.n)
            with nc.Block("blkD") as blkD:
                em.register(blkD)

    return nc


def _prep_inputs(inputs):
    x = np.asarray(inputs["L0_attn_x"])
    wqkv = np.asarray(inputs["L0_attn_w_qkv"])
    wout = np.asarray(inputs["L0_attn_w_out"])
    dp = np.asarray(inputs["L0_attn_decay_param"])
    gw = np.asarray(inputs["L0_ffn_gate_w"])
    w1 = np.asarray(inputs["L0_ffn_w1"])
    w2 = np.asarray(inputs["L0_ffn_w2"])
    w3 = np.asarray(inputs["L0_ffn_w3"])
    sw1 = np.asarray(inputs["L0_ffn_sw1"])
    sw2 = np.asarray(inputs["L0_ffn_sw2"])
    sw3 = np.asarray(inputs["L0_ffn_sw3"])

    def tile_w13(a):
        return np.ascontiguousarray(
            a.reshape(NFC, 128, NKC, 128).transpose(0, 3, 2, 1))
    wff = np.empty((NE, NFC, 128, NKC, 2, 128), dtype=np.float32)
    for e in range(16):
        wff[e, :, :, :, 0, :] = tile_w13(w1[e])
        wff[e, :, :, :, 1, :] = tile_w13(w3[e])
    wff[16, :, :, :, 0, :] = tile_w13(sw1[0:1408])
    wff[16, :, :, :, 1, :] = tile_w13(sw3[0:1408])
    wff[17, :, :, :, 0, :] = tile_w13(sw1[1408:2816])
    wff[17, :, :, :, 1, :] = tile_w13(sw3[1408:2816])
    wff = wff.reshape(NE, NFC, 128, NKC * 2 * 128)
    wff_parts = {f"wff{e}": np.ascontiguousarray(wff[e]) for e in range(NE)}

    def tile_w2(a):
        return a.reshape(4, 512, NFC, 128).transpose(3, 0, 2, 1)
    w2t = np.empty((NE, 128, 4, NFC, 512), dtype=np.float32)
    for e in range(16):
        w2t[e] = tile_w2(w2[e])
    w2t[16] = tile_w2(sw2[:, 0:1408])
    w2t[17] = tile_w2(sw2[:, 1408:2816])
    w2t = w2t.reshape(NE, 128, 4 * NFC * 512)
    w2t_parts = {f"w2t{e}": np.ascontiguousarray(w2t[e]) for e in range(NE)}

    gwt = np.ascontiguousarray(
        gw.T.reshape(NKC, 128, 16).transpose(1, 0, 2).reshape(128, NKC * 16))

    ident = np.eye(128, dtype=ml_dtypes.bfloat16)
    onec = np.ones((128, 1), dtype=ml_dtypes.bfloat16)
    oner = np.ones((1, 128), dtype=ml_dtypes.bfloat16)

    in_maps = []
    for c in range(8):
        b, p = divmod(c, 2)
        wq = np.empty((HLOC, NKC, 128, 384), dtype=np.float32)
        for h in range(HLOC):
            g = p * HLOC + h
            blk = np.concatenate(
                [wqkv[:, g * 128:(g + 1) * 128],
                 wqkv[:, 2048 + g * 128:2048 + (g + 1) * 128],
                 wqkv[:, 4096 + g * 128:4096 + (g + 1) * 128]], axis=1)
            wq[h] = blk.reshape(NKC, 128, 384)
        wq = np.ascontiguousarray(
            wq.transpose(0, 2, 1, 3).reshape(HLOC, 128, NKC * 384))
        wo = wout[p * 1024:(p + 1) * 1024]
        wo = np.ascontiguousarray(
            wo.reshape(NKC_C, 128, D).transpose(1, 0, 2).reshape(128, NKC_C * D))
        dpc = np.ascontiguousarray(
            np.tile(dp[p * HLOC:(p + 1) * HLOC][None, :], (128, 1))
        ).astype(np.float32)
        m = {
            "xa": np.ascontiguousarray(x[b]),
            "xc": np.ascontiguousarray(x[b, p * THALF:(p + 1) * THALF]),
            "wqkv": wq,
            "wout": wo,
            "gatew": gwt,
            "dp": dpc,
            "identc": ident,
            "onec": onec,
            "oner": oner,
        }
        m.update(wff_parts)
        m.update(w2t_parts)
        in_maps.append(m)
    return in_maps


def kernel(**inputs):
    in_maps = _prep_inputs(inputs)
    nc = build_nc(debug=False)
    res = run_bass_kernel_spmd(nc, in_maps, core_ids=list(range(8)))
    out = np.empty((B, T, D), dtype=np.float32)
    for c in range(8):
        b, p = divmod(c, 2)
        out[b, p * THALF:(p + 1) * THALF] = res.results[c]["out"]
    return out
